# version 13
# baseline (speedup 1.0000x reference)
import contextlib
import numpy as np
import concourse.bass as bass
import concourse.mybir as mybir

F32 = mybir.dt.float32
BF16 = mybir.dt.bfloat16
AF = mybir.ActivationFunctionType
ALU = mybir.AluOpType
AX = mybir.AxisListType

D = 1024
T = 2048
NCH = 8
NEG = -30000.0

ENGS = ("tensor", "vector", "scalar", "gpsimd", "sync")
SELF_EDGES = False
FIFO_ENGS = ("tensor", "vector")
DQ = {"qs": ("sync", 8), "qg": ("gpsimd", 6), "qt": ("sync", 2)}


def _uniquify(nc):
    if getattr(nc, "_uniq_wrapped", False):
        return
    nc._uniq_wrapped = True
    seen = {}

    def wrap(orig):
        def f(name, shape, dt, *a, **k):
            n = seen.get(name, 0)
            seen[name] = n + 1
            nm = name if n == 0 else f"{name}__{n}"
            return orig(nm, shape, dt, *a, **k)
        return f
    nc.sbuf_tensor = wrap(nc.sbuf_tensor)
    nc.psum_tensor = wrap(nc.psum_tensor)


class Sync:

    def __init__(self, nc, stack):
        _uniquify(nc)
        self.force_raw = True
        self.nc = nc
        self.sem = {}
        self.cnt = {}
        for e in ENGS:
            self.sem[e] = stack.enter_context(nc.semaphore(f"os_{e}"))
            self.cnt[e] = 0
        self.qsub = {}
        self.qrr = {}
        for q, (eng, r) in DQ.items():
            self.qsub[q] = []
            self.qrr[q] = 0
            for i in range(r):
                s = f"{q}{i}"
                self.sem[s] = stack.enter_context(nc.semaphore(f"od_{s}"))
                self.cnt[s] = 0
                self.qsub[q].append(s)
        self.streams = list(ENGS) + [s for q in DQ for s in self.qsub[q]]
        self.wm = {a: {b: 0 for b in self.streams} for a in ENGS}
        self.writers = {}
        self.readers = {}

    def _eng(self, e):
        return getattr(self.nc, e)

    def _need_wait(self, eng, dep_s, dep_c):
        if dep_c <= 0:
            return False
        return self.wm[eng][dep_s] < dep_c

    def _emit_wait(self, eng, dep_s, dep_c):
        if self._need_wait(eng, dep_s, dep_c):
            self._eng(eng).wait_ge(self.sem[dep_s], dep_c)
            self.wm[eng][dep_s] = dep_c

    @staticmethod
    def _names(aps):
        out = []
        for a in aps:
            if a is None:
                continue
            t = getattr(a, "tensor", a)
            nm = t.name if hasattr(t, "name") else str(t)
            out.append((nm, type(t).__name__ == "PSumTensorHandle"))
        return out

    def _skip(self, eng, s, skip_self):
        if s != eng:
            return False
        if skip_self or s == "tensor":
            return True
        if getattr(self, "force_raw", False):
            return False
        return (not SELF_EDGES) and s in FIFO_ENGS

    def _pre(self, eng, reads, writes, skip_self=False):
        for (t, is_ps) in self._names(reads):
            for w in self.writers.get(t, ()):
                s, c = w[0], w[1]
                fragile = len(w) > 2 and w[2]
                if (not fragile) and self._skip(eng, s, skip_self):
                    continue
                self._emit_wait(eng, s, c)
            if is_ps:
                for (s, c) in self.readers.get(t, ()):
                    if s != eng and s in ("vector", "scalar"):
                        self._emit_wait(eng, s, c)
        for (t, is_ps) in self._names(writes):
            for w in self.writers.get(t, ()):
                s, c = w[0], w[1]
                if s == eng and (skip_self or s == "tensor"):
                    continue
                self._emit_wait(eng, s, c)
            for (s, c) in self.readers.get(t, ()):
                if s == eng and (skip_self or s == "tensor"):
                    continue
                self._emit_wait(eng, s, c)

    def _post(self, stream, reads, writes, n, fragile=False):
        for (t, _) in self._names(writes):
            self.writers[t] = [(stream, n, fragile)]
            self.readers[t] = []
        for (t, _) in self._names(reads):
            self.readers.setdefault(t, []).append((stream, n))

    def op(self, eng, fn, reads=(), writes=(), skip_self=False, fragile=False):
        self._pre(eng, reads, writes, skip_self=skip_self)
        inst = fn(self._eng(eng))
        self.cnt[eng] += 1
        inst.then_inc(self.sem[eng], 1)
        self._post(eng, reads, writes, self.cnt[eng], fragile=fragile)
        return inst

    def dma(self, q, out, in_, transpose=False, accum=False, wkey=None, rkey=None):
        eng = DQ[q][0]
        sub = self.qsub[q][self.qrr[q] % len(self.qsub[q])]
        self.qrr[q] += 1
        self._emit_wait(eng, sub, self.cnt[sub])
        rd = [rkey] if rkey is not None else [in_]
        wr = [wkey] if wkey is not None else [out]
        self._pre(eng, rd, wr)
        e = self._eng(eng)
        if transpose:
            inst = e.dma_start_transpose(out, in_)
        elif accum:
            inst = e.dma_start(out, in_, accum_op=ALU.add)
        else:
            inst = e.dma_start(out, in_)
        self.cnt[sub] += 16
        inst.then_inc(self.sem[sub], 16)
        self._post(sub, rd, wr, self.cnt[sub])
        return inst

    def barrier(self):
        nc = self.nc
        subs = [s for q in DQ for s in self.qsub[q]]
        for s in subs:
            self._emit_wait("sync", s, self.cnt[s])
        nc.all_engine_barrier()
        for a in ENGS:
            for b in ENGS:
                self.wm[a][b] = self.cnt[b]
            for s in subs:
                self.wm[a][s] = self.cnt[s]

    def mm(self, out, lhsT, rhs, start=True, stop=True, tile_position=None):
        return self.op("tensor",
                       lambda e: e.matmul(out, lhsT, rhs, start=start, stop=stop,
                                          tile_position=tile_position),
                       reads=[lhsT, rhs], writes=[out], skip_self=True)

    def act(self, out, in_, func, bias=0.0, scale=1.0, accum_out=None):
        rd = [in_]
        if isinstance(bias, bass.AP):
            rd.append(bias)
        if isinstance(scale, bass.AP):
            rd.append(scale)
        wr = [out] + ([accum_out] if accum_out is not None else [])
        return self.op("scalar",
                       lambda e: e.activation(out, in_, func, bias=bias, scale=scale,
                                              accum_out=accum_out),
                       reads=rd, writes=wr)

    def copy(self, eng, out, in_):
        return self.op(eng, lambda e: e.tensor_copy(out, in_), reads=[in_], writes=[out])

    def scopy(self, out, in_):
        return self.op("scalar", lambda e: e.copy(out, in_), reads=[in_], writes=[out])

    def tt(self, out, in0, in1, alu, eng="vector"):
        return self.op(eng, lambda e: e.tensor_tensor(out, in0, in1, alu),
                       reads=[in0, in1], writes=[out])

    def ts(self, out, in0, s1, s2, op0, op1=None, eng="vector"):
        rd = [in0] + [x for x in (s1, s2) if isinstance(x, bass.AP)]
        if op1 is None:
            return self.op(eng, lambda e: e.tensor_single_scalar(out, in0, s1, op0),
                           reads=rd, writes=[out])
        return self.op(eng, lambda e: e.tensor_scalar(out, in0, s1, s2, op0, op1),
                       reads=rd, writes=[out])

    def stt(self, out, in0, scalar, in1, op0, op1, eng="vector"):
        rd = [in0, in1] + ([scalar] if isinstance(scalar, bass.AP) else [])
        return self.op(eng, lambda e: e.scalar_tensor_tensor(out, in0, scalar, in1, op0, op1),
                       reads=rd, writes=[out])

    def red(self, out, in_, alu, eng="vector", negate=False):
        return self.op(eng, lambda e: e.tensor_reduce(out, in_, AX.X, alu, negate=negate),
                       reads=[in_], writes=[out], fragile=True)

    def memset(self, out, val, eng="vector"):
        return self.op(eng, lambda e: e.memset(out, val), reads=[], writes=[out])

    def recip(self, out, in_):
        return self.op("vector", lambda e: e.reciprocal(out, in_), reads=[in_], writes=[out],
                       fragile=True)

    def rowpow(self, buf, scale):
        self.act(buf, buf, AF.Ln)
        self.act(buf, buf, AF.Exp, scale=scale)


class Consts:

    def __init__(self, nc, S, stack, dts):
        sb = lambda n, sh, dt: stack.enter_context(nc.sbuf_tensor(n, sh, dt))
        self.ones_col_bf = sb("k_ones_col_bf", [128, 1], BF16)
        self.ones_row_bf = sb("k_ones_row_bf", [1, 512], BF16)
        self.ones_row_f = sb("k_ones_row_f", [1, 128], F32)
        self.t_lo = sb("k_t_lo", [128, 128], F32)
        self.t_up = sb("k_t_up", [128, 128], F32)
        self.ident_bf = sb("k_ident_bf", [128, 128], BF16)
        self.ident_f = sb("k_ident_f", [128, 128], F32)
        self.ones_col_f = sb("k_ones_col_f", [128, 1], F32)
        S.memset(self.ones_col_bf[:], 1.0)
        S.memset(self.ones_col_f[:], 1.0)
        S.memset(self.ones_row_bf[:], 1.0)
        S.memset(self.ones_row_f[:], 1.0)
        S.dma("qs", self.t_lo[:], dts["c_t_lo"].ap())
        S.dma("qs", self.t_up[:], dts["c_t_up"].ap())
        S.dma("qs", self.ident_bf[:], dts["c_ident"].ap())
        S.dma("qs", self.ident_f[:], dts["c_ident_f"].ap())


def alloc_chunks(nc, stack, name, nch, shape, dt):
    return [stack.enter_context(nc.sbuf_tensor(f"{name}{i}", shape, dt))
            for i in range(nch)]


def emit_ln(nc, S, C, stack, xin, out_chunks, out_dt, tile=512, out_dram=None):
    with contextlib.ExitStack() as st:
        xs = alloc_chunks(nc, st, "ln_x", NCH, [128, tile], F32)
        xs2 = alloc_chunks(nc, st, "ln_x2", NCH, [128, tile], F32)
        xb = alloc_chunks(nc, st, "ln_xb", 2, [128, tile], BF16)
        sq = alloc_chunks(nc, st, "ln_sq", 2, [128, tile], BF16)
        tmp = alloc_chunks(nc, st, "ln_tmp", 2, [128, tile], F32)
        tmp2 = alloc_chunks(nc, st, "ln_tmp2", 2, [128, tile], F32) if out_dram is not None else None
        rowa2 = [st.enter_context(nc.sbuf_tensor(f"ln_rowa{i}", [1, tile], F32)) for i in range(2)]
        rowb2 = [st.enter_context(nc.sbuf_tensor(f"ln_rowb{i}", [1, tile], F32)) for i in range(2)]
        rowc2 = [st.enter_context(nc.sbuf_tensor(f"ln_rowc{i}", [1, tile], F32)) for i in range(2)]
        ps_s2 = [st.enter_context(nc.psum_tensor(f"ln_ps_s{i}", [1, tile], F32)) for i in range(2)]
        ps_q2 = [st.enter_context(nc.psum_tensor(f"ln_ps_q{i}", [1, tile], F32)) for i in range(2)]
        ps_a2 = [st.enter_context(nc.psum_tensor(f"ln_ps_a{i}", [128, tile], F32)) for i in range(2)]
        ps_b2 = [st.enter_context(nc.psum_tensor(f"ln_ps_b{i}", [128, tile], F32)) for i in range(2)]
        ntile = T // tile
        S.force_raw = False
        for t in range(ntile):
            rowa, rowb, rowc = rowa2[t % 2], rowb2[t % 2], rowc2[t % 2]
            ps_s, ps_q, ps_a, ps_b = ps_s2[t % 2], ps_q2[t % 2], ps_a2[t % 2], ps_b2[t % 2]
            xcur = xs if t % 2 == 0 else xs2
            for c in range(NCH):
                S.dma("qs", xcur[c][:], xin[c * 128:(c + 1) * 128, t * tile:(t + 1) * tile])
            for c in range(NCH):
                b = xb[c % 2]
                s = sq[c % 2]
                S.copy("vector", b[:], xcur[c][:])
                S.tt(s[:], b[:], b[:], ALU.mult, eng="gpsimd")
                S.mm(ps_s[:], C.ones_col_bf[:], b[:], start=(c == 0), stop=(c == NCH - 1))
                S.mm(ps_q[:], C.ones_col_bf[:], s[:], start=(c == 0), stop=(c == NCH - 1))
            S.ts(rowa[:], ps_s[:], 1.0 / D, None, ALU.mult)
            S.ts(rowb[:], ps_q[:], 1.0 / D, None, ALU.mult)
            S.tt(rowc[:], rowa[:], rowa[:], ALU.mult)
            S.tt(rowb[:], rowb[:], rowc[:], ALU.subtract)
            S.ts(rowb[:], rowb[:], 1e-5, None, ALU.add)
            S.rowpow(rowb[:], -0.5)
            S.stt(rowc[:], rowa[:], -1.0, rowb[:], ALU.mult, ALU.mult)
            S.mm(ps_a[:], C.ones_row_f[:], rowb[:])
            S.mm(ps_b[:], C.ones_row_f[:], rowc[:])
            for c in range(NCH):
                tm = tmp[c % 2]
                S.tt(tm[:], xcur[c][:], ps_a[:], ALU.mult)
                if out_dram is None:
                    S.tt(out_chunks[c][:, t * tile:(t + 1) * tile], tm[:], ps_b[:], ALU.add)
                else:
                    to = tmp2[c % 2]
                    S.tt(to[:], tm[:], ps_b[:], ALU.add)
                    S.dma("qs", out_dram[c * 128:(c + 1) * 128, t * tile:(t + 1) * tile], to[:])
        S.force_raw = True
        S.barrier()



def host_constants():
    import ml_dtypes
    i = np.arange(128)
    t_lo = np.where(i[:, None] > i[None, :], np.float32(NEG), np.float32(0.0)).astype(np.float32)
    t_up = t_lo.T.copy()
    ident = np.eye(128).astype(ml_dtypes.bfloat16)
    return {"c_t_lo": t_lo, "c_t_up": t_up, "c_ident": ident,
            "c_ident_f": np.eye(128, dtype=np.float32)}



def emit_l0_attn(nc, S, C, stack, xin, xout, W):
    NH = 16
    GH = 2
    NGR = NH // GH

    with contextlib.ExitStack() as st:
        xn = alloc_chunks(nc, st, "a0_xn", NCH, [128, T], BF16)
        emit_ln(nc, S, C, st, xin, xn, BF16)

        bbf = {}
        for nm, key in (("q", "L0_attn_bq"), ("k", "L0_attn_bk"), ("v", "L0_attn_bv")):
            bc = st.enter_context(nc.sbuf_tensor(f"a0_b{nm}c", [128, 8], F32))
            bbf[nm] = st.enter_context(nc.sbuf_tensor(f"a0_b{nm}b", [128, 8], BF16))
            for c in range(8):
                S.dma("qs", bc[:, c], W[key].ap()[c * 128:(c + 1) * 128])
            S.scopy(bbf[nm][:], bc[:])
        cfl = st.enter_context(nc.sbuf_tensor("a0_cfl", [1, 32], F32))
        vbo = st.enter_context(nc.sbuf_tensor("a0_vbo", [1, 2080], F32))
        S.dma("qs", cfl[:], W["cfl"].ap())
        S.dma("qs", vbo[:], W["vb_ones"].ap())

        o_fm = alloc_chunks(nc, st, "a0_o", NCH, [128, T], BF16)

        S.force_raw = False
        with contextlib.ExitStack() as gs:
            a_bf = alloc_chunks(nc, gs, "a0_a", NCH, [128, 130], BF16)
            weq = alloc_chunks(nc, gs, "a0_weq", NCH, [128, 256], BF16)
            wek = alloc_chunks(nc, gs, "a0_wek", NCH, [128, 256], BF16)
            wev = alloc_chunks(nc, gs, "a0_wev", NCH, [128, 260], BF16)
            beqs = gs.enter_context(nc.sbuf_tensor("a0_beqs", [128, 2], F32))
            bek = gs.enter_context(nc.sbuf_tensor("a0_bek", [128, 2], F32))
            bvr = gs.enter_context(nc.sbuf_tensor("a0_bvr", [1, 260], BF16))
            q_aug = [[gs.enter_context(nc.sbuf_tensor(f"a0_q{h}_{p}", [65, T], BF16))
                      for p in range(2)] for h in range(GH)]
            k_aug = [gs.enter_context(nc.sbuf_tensor(f"a0_k{h}", [65, 2 * T], BF16))
                     for h in range(GH)]
            v_sb = alloc_chunks(nc, gs, "a0_v", 16, [128, 260], BF16)
            sqq = [gs.enter_context(nc.sbuf_tensor(f"a0_sqq{i}", [128, 512], BF16)) for i in range(2)]
            nrow = gs.enter_context(nc.sbuf_tensor("a0_nr", [1, 4096], F32))
            maxk = [gs.enter_context(nc.sbuf_tensor(f"a0_mk{h}", [1, 1], F32)) for h in range(GH)]
            pb = [gs.enter_context(nc.sbuf_tensor(f"a0_pb{i}", [128, 512], BF16)) for i in range(4)]
            bcs = [gs.enter_context(nc.sbuf_tensor(f"a0_bcs{i}", [64, 512], F32)) for i in range(2)]
            tmp0 = gs.enter_context(nc.sbuf_tensor("a0_tmp0", [64, 512], F32))
            tmp2 = gs.enter_context(nc.sbuf_tensor("a0_tmp2", [64, 512], F32))
            oraw = [gs.enter_context(nc.sbuf_tensor(f"a0_oraw{i}", [64, T], BF16)) for i in range(2)]

            ps_f = [gs.enter_context(nc.psum_tensor(f"a0_psf{i}", [128, 512], F32)) for i in range(2)]
            ps_misc = gs.enter_context(nc.psum_tensor("a0_psmisc", [128, 512], F32))
            ps_b = ps_misc
            ps_n = ps_misc
            ps_s = [gs.enter_context(nc.psum_tensor(f"a0_pss{i}", [128, 512], F32)) for i in range(2)]
            ps_avs = [gs.enter_context(nc.psum_tensor(f"a0_psav{i}", [65, 512], F32)) for i in range(2)]
            ps_bc = gs.enter_context(nc.psum_tensor("a0_psbc", [64, 512], F32))

            weq_d = nc.dram_tensor("a0_weq_scr", [1024, 2048], BF16).ap()
            wek_d = nc.dram_tensor("a0_wek_scr", [1024, 2048], BF16).ap()
            wev_d = nc.dram_tensor("a0_wev_scr", [1024, 2080], BF16).ap()
            with contextlib.ExitStack() as fs_:
                w_bf = alloc_chunks(nc, fs_, "a0_w", NCH, [128, 1024], BF16)
                a_big = alloc_chunks(nc, fs_, "a0_abig", NCH, [128, 512], BF16)
                fout = [fs_.enter_context(nc.sbuf_tensor(f"a0_fo{i}", [128, 512], BF16))
                        for i in range(2)]
                for (wkey, akey, wd, wtot) in (("L0_attn_Wq", "A_q", weq_d, 2048),
                                               ("L0_attn_Wk", "A_k", wek_d, 2048),
                                               ("L0_attn_Wv", "A_v", wev_d, 2080)):
                    for r in range(NCH):
                        S.dma("qg", w_bf[r][:], W[wkey].ap()[r * 128:(r + 1) * 128, :])
                    for nt in range((wtot + 511) // 512):
                        nw = min(512, wtot - nt * 512)
                        for r in range(NCH):
                            S.dma("qg", a_big[r][:, 0:nw],
                                  W[akey].ap()[r * 128:(r + 1) * 128, nt * 512:nt * 512 + nw])
                        for m in range(NCH):
                            for r in range(NCH):
                                S.mm(ps_f[m % 2][:, 0:nw], w_bf[r][:, m * 128:(m + 1) * 128],
                                     a_big[r][:, 0:nw], start=(r == 0), stop=(r == NCH - 1))
                            fo = fout[m % 2]
                            S.copy("vector", fo[:, 0:nw], ps_f[m % 2][:, 0:nw])
                            S.dma("qs", wd[m * 128:(m + 1) * 128, nt * 512:nt * 512 + nw],
                                  fo[:, 0:nw])
                S.barrier()

            for g in range(NGR):
                for (akey, we, wid, bias_nm) in (("A_q", weq, 128, "q"),
                                                 ("A_k", wek, 128, "k"),
                                                 ("A_v", wev, 130, "v")):
                    stride = 1024 if wid == 128 else 1040
                    wd = {"q": weq_d, "k": wek_d, "v": wev_d}[bias_nm]
                    for p in range(2):
                        base = p * stride + g * wid
                        for r in range(NCH):
                            S.dma("qs", we[r][:, p * wid:(p + 1) * wid],
                                  wd[r * 128:(r + 1) * 128, base:base + wid])
                        for r in range(NCH):
                            S.dma("qg", a_bf[r][:, 0:wid],
                                  W[akey].ap()[r * 128:(r + 1) * 128, base:base + wid])
                        if bias_nm in ("q", "k"):
                            bout = beqs if bias_nm == "q" else bek
                            for r in range(NCH):
                                S.mm(ps_b[:, 0:1], a_bf[r][:, 0:128],
                                     bbf[bias_nm][:, r:r + 1],
                                     start=(r == 0), stop=(r == NCH - 1))
                            sc = 0.125 if bias_nm == "q" else 1.0
                            S.op("scalar", lambda e, o=bout[:, p:p + 1],
                                 i=ps_b[:, 0:1], s=sc: e.mul(o, i, s),
                                 reads=[ps_b], writes=[bout])
                        else:
                            for r in range(NCH):
                                S.mm(ps_n[0:1, 0:130], bbf["v"][:, r:r + 1], a_bf[r][:, 0:130],
                                     start=(r == 0), stop=(r == NCH - 1))
                            S.tt(bvr[:, p * 130:(p + 1) * 130], ps_n[0:1, 0:130],
                                 vbo[:, p * 1040 + g * 130: p * 1040 + (g + 1) * 130], ALU.add)

                for p in range(2):
                    for tc in range(16):
                        pv = ps_f[tc % 2]
                        for m in range(NCH):
                            S.mm(pv[:, 0:130], xn[m][:, tc * 128:(tc + 1) * 128],
                                 wev[m][:, p * 130:(p + 1) * 130],
                                 start=(m == 0), stop=False)
                        S.mm(pv[:, 0:130], C.ones_row_bf[0:1, 0:128],
                             bvr[0:1, p * 130:(p + 1) * 130], start=False, stop=True)
                        S.copy("vector", v_sb[tc][:, p * 130:(p + 1) * 130], pv[:, 0:130])

                for hh in range(GH):
                    S.memset(k_aug[hh][64:65, :], 1.0)
                for p in range(2):
                    for t4 in range(4):
                        pk_ = ps_f[t4 % 2]
                        for m in range(NCH):
                            S.mm(pk_[:, 0:512], wek[m][:, p * 128:(p + 1) * 128],
                                 xn[m][:, t4 * 512:(t4 + 1) * 512],
                                 start=(m == 0), stop=(m == NCH - 1))
                        for hh in range(GH):
                            S.ts(k_aug[hh][0:64, p * T + t4 * 512: p * T + (t4 + 1) * 512],
                                 pk_[hh * 64:(hh + 1) * 64, 0:512],
                                 bek[hh * 64:(hh + 1) * 64, p:p + 1], None, ALU.add)
                for hh in range(GH):
                    for p in range(2):
                        for t4 in range(4):
                            kslc = k_aug[hh][0:64, p * T + t4 * 512: p * T + (t4 + 1) * 512]
                            S.tt(sqq[0][0:64, 0:512], kslc, kslc, ALU.mult, eng="gpsimd")
                            S.mm(ps_n[0:1, 0:512], C.ones_col_bf[0:64, :], sqq[0][0:64, 0:512])
                            S.copy("vector", nrow[:, p * T + t4 * 512: p * T + (t4 + 1) * 512],
                                   ps_n[0:1, 0:512])
                    S.rowpow(nrow[:], 0.5)
                    S.red(maxk[hh][:], nrow[:], ALU.max)

                for p in range(2):
                    for t4 in range(4):
                        pq_ = ps_f[t4 % 2]
                        for m in range(NCH):
                            S.mm(pq_[:, 0:512], weq[m][:, p * 128:(p + 1) * 128],
                                 xn[m][:, t4 * 512:(t4 + 1) * 512],
                                 start=(m == 0), stop=(m == NCH - 1))
                        for hh in range(GH):
                            S.ts(q_aug[hh][p][0:64, t4 * 512:(t4 + 1) * 512],
                                 pq_[hh * 64:(hh + 1) * 64, 0:512],
                                 0.125, beqs[hh * 64:(hh + 1) * 64, p:p + 1],
                                 ALU.mult, ALU.add)
                    for hh in range(GH):
                        for t4 in range(4):
                            qslc = q_aug[hh][p][0:64, t4 * 512:(t4 + 1) * 512]
                            S.tt(sqq[t4 % 2][0:64, 0:512], qslc, qslc, ALU.mult, eng="gpsimd")
                            S.mm(ps_n[0:1, 0:512], C.ones_col_bf[0:64, :], sqq[t4 % 2][0:64, 0:512])
                            S.copy("vector", nrow[:, t4 * 512:(t4 + 1) * 512], ps_n[0:1, 0:512])
                        S.rowpow(nrow[0:1, 0:T], 0.5)
                        S.ts(q_aug[hh][p][64:65, :], nrow[0:1, 0:T], maxk[hh][0:1, 0:1], -1.0,
                             ALU.mult, ALU.mult)

                sbanks = [ps_s[0], ps_s[1], ps_f[0], ps_f[1]]
                for hh in range(GH):
                    h = g * GH + hh
                    for p in range(2):
                        orawp = oraw[p]
                        for qt in range(4):
                            ps_av = ps_avs[qt % 2]
                            qsl = q_aug[hh][p][:, qt * 512:(qt + 1) * 512]
                            chunks = [(pk, c) for pk in range(2) for c in range(4 * qt + 4)]
                            nck = len(chunks)

                            def emit_score(i):
                                pk, c = chunks[i]
                                cr = c - 4 * qt
                                n0 = cr * 128 if cr > 0 else 0
                                pss = sbanks[i % 4]
                                S.mm(pss[:, n0:512],
                                     k_aug[hh][:, pk * T + c * 128: pk * T + (c + 1) * 128],
                                     qsl[:, n0:512] if n0 else qsl)
                                if cr >= 0:
                                    S.tt(pss[:, cr * 128:(cr + 1) * 128],
                                         pss[:, cr * 128:(cr + 1) * 128], C.t_lo[:], ALU.add)
                                prb = pb[i % 4]
                                if n0:
                                    S.memset(prb[:, 0:n0], 0.0)
                                S.act(prb[:, n0:512], pss[:, n0:512], AF.Exp)

                            def emit_av(i):
                                pk, c = chunks[i]
                                S.mm(ps_av[:, 0:512],
                                     v_sb[c][:, pk * 130 + hh * 65: pk * 130 + (hh + 1) * 65],
                                     pb[i % 4][:], start=(i == 0), stop=(i == nck - 1))

                            for i in range(nck):
                                emit_score(i)
                                if i >= 2:
                                    emit_av(i - 2)
                            emit_av(nck - 2)
                            emit_av(nck - 1)
                            S.copy("vector", orawp[:, qt * 512:(qt + 1) * 512],
                                   ps_av[0:64, 0:512])
                            S.copy("vector", nrow[0:1, p * T + qt * 512: p * T + (qt + 1) * 512],
                                   ps_av[64:65, 0:512])
                        S.rowpow(nrow[0:1, p * T:(p + 1) * T], -1.0)
                        S.ts(nrow[0:1, p * T:(p + 1) * T], nrow[0:1, p * T:(p + 1) * T],
                             cfl[0:1, h * 2 + p: h * 2 + p + 1], None, ALU.mult)
                    for qt in range(4):
                        tsl = slice(qt * 512, (qt + 1) * 512)
                        for p in range(2):
                            S.mm(ps_bc[:, 0:512], C.ones_row_f[0:1, 0:64],
                                 nrow[0:1, p * T + qt * 512: p * T + (qt + 1) * 512])
                            S.copy("vector", bcs[p][:], ps_bc[:, 0:512])
                        S.tt(tmp0[:], oraw[0][:, tsl], bcs[0][:], ALU.mult)
                        S.tt(tmp2[:], oraw[1][:, tsl], bcs[1][:], ALU.mult)
                        osl = o_fm[h // 2][(h % 2) * 64:(h % 2) * 64 + 64, tsl]
                        S.tt(osl, tmp0[:], tmp2[:], ALU.add)
            S.barrier()

        with contextlib.ExitStack() as os_:
            wo = alloc_chunks(nc, os_, "a0_wo", NCH, [128, 1024], BF16)
            bo_row = os_.enter_context(nc.sbuf_tensor("a0_bor", [1, 1024], F32))
            bo_bf = os_.enter_context(nc.sbuf_tensor("a0_bob", [1, 1024], BF16))
            S.dma("qs", bo_row[0:1, :], W["L0_attn_bo"].ap()[None, :])
            S.scopy(bo_bf[:], bo_row[:])
            for r in range(NCH):
                S.dma("qg", wo[r][:], W["WoT"].ap()[r * 128:(r + 1) * 128, :])
            xt = [os_.enter_context(nc.sbuf_tensor(f"a0_xt{i}", [128, 512], F32)) for i in range(2)]
            xo = [os_.enter_context(nc.sbuf_tensor(f"a0_xo{i}", [128, 512], F32)) for i in range(2)]
            ps_o = [os_.enter_context(nc.psum_tensor(f"a0_pso{i}", [128, 512], F32)) for i in range(2)]
            for dm in range(NCH):
                for t4 in range(4):
                    pso = ps_o[(dm * 4 + t4) % 2]
                    for ic in range(NCH):
                        S.mm(pso[:, 0:512], wo[ic][:, dm * 128:(dm + 1) * 128],
                             o_fm[ic][:, t4 * 512:(t4 + 1) * 512], start=(ic == 0), stop=False)
                    S.mm(pso[:, 0:512], bo_bf[0:1, dm * 128:(dm + 1) * 128],
                         C.ones_row_bf[0:1, 0:512], start=False, stop=True)
                    i2 = (dm * 4 + t4) % 2
                    S.dma("qs", xt[i2][:], xin[dm * 128:(dm + 1) * 128, t4 * 512:(t4 + 1) * 512])
                    S.tt(xo[i2][:], pso[:, 0:512], xt[i2][:], ALU.add)
                    S.dma("qs", xout[dm * 128:(dm + 1) * 128, t4 * 512:(t4 + 1) * 512], xo[i2][:])
            S.force_raw = True
            S.barrier()


def host_l0_attn_inputs(ins):
    H, P, Dh = 16, 2, 64
    aq, ak, av = ins["L0_attn_alpha_q"], ins["L0_attn_alpha_k"], ins["L0_attn_alpha_v"]
    A_q = np.zeros((1024, 2048), np.float32)
    A_k = np.zeros((1024, 2048), np.float32)
    A_v = np.zeros((1024, 2080), np.float32)
    dd = np.arange(Dh)
    for m in range(H):
        for h in range(H):
            for p in range(P):
                A_q[m * Dh + dd, p * 1024 + h * Dh + dd] = aq[m, h, p]
                A_k[m * Dh + dd, p * 1024 + h * Dh + dd] = ak[m, h, p]
                A_v[m * Dh + dd, p * 1040 + h * 65 + dd] = av[m, h, p]
    vb = np.zeros((1, 2080), np.float32)
    for p in range(P):
        for h in range(H):
            vb[0, p * 1040 + h * 65 + 64] = 1.0
    return {
        "A_q": A_q, "A_k": A_k, "A_v": A_v,
        "vb_ones": vb,
        "cfl": np.ascontiguousarray(ins["L0_attn_collapse"].reshape(1, 32)),
        "WoT": np.ascontiguousarray(np.asarray(ins["L0_attn_Wo"]).T),
    }


def decl_l0_attn_weights(nc):
    dts = {}
    for nm, sh in (("L0_attn_Wq", (1024, 1024)), ("L0_attn_Wk", (1024, 1024)),
                   ("L0_attn_Wv", (1024, 1024)), ("WoT", (1024, 1024)),
                   ("L0_attn_bq", (1024,)), ("L0_attn_bk", (1024,)),
                   ("L0_attn_bv", (1024,)), ("L0_attn_bo", (1024,)),
                   ("A_q", (1024, 2048)), ("A_k", (1024, 2048)), ("A_v", (1024, 2080)),
                   ("vb_ones", (1, 2080)), ("cfl", (1, 32))):
        dts[nm] = nc.dram_tensor(nm, list(sh), F32, kind="ExternalInput")
    return dts



def emit_l0_ffn(nc, S, C, stack, xin, xout, W, nexp=10, flags=()):
    E = 8
    with contextlib.ExitStack() as st:
        xn = alloc_chunks(nc, st, "f0_xn", NCH, [128, T], BF16)
        emit_ln(nc, S, C, st, xin, xn, BF16)

        for c in range(NCH):
            S.dma("qs", xout[c * 128:(c + 1) * 128, :], xin[c * 128:(c + 1) * 128, :],
                  wkey=f"xoacc{c}")

        skipr = "skipr" in flags
        gate_fm = st.enter_context(nc.sbuf_tensor("f0_gfm", [8, T], F32))
        gate_bf = st.enter_context(nc.sbuf_tensor("f0_gbf", [8, T], BF16))
        with contextlib.ExitStack() as rs:
            gwT = alloc_chunks(nc, rs, "f0_gw", NCH, [128, 8], BF16)
            for r in range(NCH):
                S.dma("qg", gwT[r][:], W["gwT"].ap()[r * 128:(r + 1) * 128, :])
            gbrow = rs.enter_context(nc.sbuf_tensor("f0_gb", [1, 8], F32))
            gbb = rs.enter_context(nc.sbuf_tensor("f0_gbb", [1, 8], BF16))
            S.dma("qs", gbrow[:], W["L0_ffn_gate_b"].ap()[None, :])
            S.scopy(gbb[:], gbrow[:])
            lg = [rs.enter_context(nc.sbuf_tensor(f"f0_lg{i}", [128, 8], F32)) for i in range(2)]
            e8 = rs.enter_context(nc.sbuf_tensor("f0_e8", [128, 8], F32))
            keep = rs.enter_context(nc.sbuf_tensor("f0_keep", [128, 8], F32))
            l2 = rs.enter_context(nc.sbuf_tensor("f0_l2", [128, 8], F32))
            m1 = rs.enter_context(nc.sbuf_tensor("f0_m1", [128, 1], F32))
            m2 = rs.enter_context(nc.sbuf_tensor("f0_m2", [128, 1], F32))
            gt = rs.enter_context(nc.sbuf_tensor("f0_gt", [128, 8], F32))
            ps_l = [rs.enter_context(nc.psum_tensor(f"f0_psl{i}", [128, 8], F32)) for i in range(2)]
            ps_t = [rs.enter_context(nc.psum_tensor(f"f0_pst{i}", [8, 128], F32)) for i in range(2)]
            for tc in range(0 if skipr else 16):
                i = tc % 2
                pl = ps_l[i]
                for m in range(NCH):
                    S.mm(pl[:], xn[m][:, tc * 128:(tc + 1) * 128], gwT[m][:],
                         start=(m == 0), stop=False)
                S.mm(pl[:], C.ones_row_bf[0:1, 0:128], gbb[:], start=False, stop=True)
                L = lg[i]
                S.copy("vector", L[:], pl[:])
                if "r1" in flags:
                    continue
                S.red(m1[:], L[:], ALU.max)
                S.ts(keep[:], L[:], m1[:], None, ALU.is_ge)
                S.stt(l2[:], keep[:], NEG, L[:], ALU.mult, ALU.add)
                S.red(m2[:], l2[:], ALU.max)
                S.ts(keep[:], L[:], m2[:], None, ALU.is_ge)
                S.ts(m1[:], m1[:], -1.0, None, ALU.mult)
                if "r2" in flags:
                    continue
                S.act(e8[:], L[:], AF.Exp, bias=m1[:])
                S.red(m2[:], e8[:], ALU.add)
                S.recip(m2[:], m2[:])
                S.tt(gt[:], e8[:], keep[:], ALU.mult)
                S.ts(gt[:], gt[:], m2[:], None, ALU.mult)
                if "r3" in flags:
                    continue
                S.op("tensor", lambda e, o=ps_t[i][:], a=gt[:], idn=C.ident_f[:]:
                     e.transpose(o, a, idn), reads=[gt, C.ident_f], writes=[ps_t[i]])
                if "r4" in flags:
                    continue
                S.copy("vector", gate_fm[:, tc * 128:(tc + 1) * 128], ps_t[i][:])
                if "r5" in flags:
                    continue
                S.copy("gpsimd", gate_bf[:, tc * 128:(tc + 1) * 128],
                       gate_fm[:, tc * 128:(tc + 1) * 128])
            S.barrier()

        we1s = [alloc_chunks(nc, st, f"f0_we1{i}", NCH, [128, 1024], BF16) for i in range(2)]
        we3s = [alloc_chunks(nc, st, f"f0_we3{i}", NCH, [128, 1024], BF16) for i in range(2)]
        we2s = [alloc_chunks(nc, st, f"f0_we2{i}", NCH, [128, 1024], BF16) for i in range(2)]
        b1cs = [st.enter_context(nc.sbuf_tensor(f"f0_b1c{i}", [128, 8], F32)) for i in range(2)]
        b3cs = [st.enter_context(nc.sbuf_tensor(f"f0_b3c{i}", [128, 8], F32)) for i in range(2)]
        yy = [st.enter_context(nc.sbuf_tensor(f"f0_yy{i}", [128, 512], F32)) for i in range(4)]
        hh = alloc_chunks(nc, st, "f0_hh", NCH, [128, 512], BF16)
        ge_sb = st.enter_context(nc.sbuf_tensor("f0_ge", [128, T], F32))
        grow = st.enter_context(nc.sbuf_tensor("f0_grow", [1, T], F32))
        sa = [st.enter_context(nc.sbuf_tensor(f"f0_sa{i}", [128, 512], BF16)) for i in range(4)]
        bb = [st.enter_context(nc.sbuf_tensor(f"f0_bb{i}", [128, 512], BF16)) for i in range(4)]
        b2a = st.enter_context(nc.sbuf_tensor("f0_b2a", [8, 1024], BF16))
        bs2b = st.enter_context(nc.sbuf_tensor("f0_bs2", [1, 1024], BF16))
        ps_as = [st.enter_context(nc.psum_tensor(f"f0_psa{i}", [128, 512], F32)) for i in range(2)]
        ps_bs = [st.enter_context(nc.psum_tensor(f"f0_psb{i}", [128, 512], F32)) for i in range(2)]
        ps_y = [st.enter_context(nc.psum_tensor(f"f0_psy{i}", [128, 512], F32)) for i in range(2)]
        ps_g = st.enter_context(nc.psum_tensor("f0_psg", [128, 512], F32))

        if "skipb" not in flags:
            S.dma("qg", b2a[:], W["L0_ffn_b2"].ap())
            S.dma("qg", bs2b[:], W["bs2"].ap()[None, :])

        S.force_raw = False
        for e in range(nexp):
            shared = e >= E
            sh = e - E
            we1, we3, we2 = we1s[e % 2], we3s[e % 2], we2s[e % 2]
            b1c, b3c = b1cs[e % 2], b3cs[e % 2]
            for r in range(NCH):
                if not shared:
                    S.dma("qg", we1[r][:], W["w1T"].ap()[e, r * 128:(r + 1) * 128, :])
                    S.dma("qg", we3[r][:], W["w3T"].ap()[e, r * 128:(r + 1) * 128, :])
                    S.dma("qg", we2[r][:], W["w2T"].ap()[e, r * 128:(r + 1) * 128, :])
                else:
                    S.dma("qg", we1[r][:], W["ws1T"].ap()[r * 128:(r + 1) * 128,
                                                          sh * 1024:(sh + 1) * 1024])
                    S.dma("qg", we3[r][:], W["ws3T"].ap()[r * 128:(r + 1) * 128,
                                                          sh * 1024:(sh + 1) * 1024])
                    S.dma("qg", we2[r][:], W["ws2T"].ap()[sh * 1024 + r * 128:
                                                          sh * 1024 + (r + 1) * 128, :])
            for c in range(NCH):
                if not shared:
                    S.dma("qs", b1c[:, c], W["L0_ffn_b1"].ap()[e, c * 128:(c + 1) * 128])
                    S.dma("qs", b3c[:, c], W["L0_ffn_b3"].ap()[e, c * 128:(c + 1) * 128])
                else:
                    S.dma("qs", b1c[:, c], W["bs1"].ap()[sh * 1024 + c * 128: sh * 1024 + (c + 1) * 128])
                    S.dma("qs", b3c[:, c], W["bs3"].ap()[sh * 1024 + c * 128: sh * 1024 + (c + 1) * 128])
            if not shared:
                S.dma("qs", grow[0:1, :], gate_fm[e:e + 1, :])
                for t4 in range(4):
                    S.mm(ps_g[:, 0:512], C.ones_row_f[0:1, 0:128],
                         grow[0:1, t4 * 512:(t4 + 1) * 512])
                    S.copy("vector", ge_sb[:, t4 * 512:(t4 + 1) * 512], ps_g[:, 0:512])
            for t4 in range(4):
                tsl = slice(t4 * 512, (t4 + 1) * 512)
                for ic in range(NCH):
                    ps_a, ps_bp = ps_as[ic % 2], ps_bs[ic % 2]
                    for m in range(NCH):
                        S.mm(ps_a[:, 0:512], we1[m][:, ic * 128:(ic + 1) * 128],
                             xn[m][:, tsl], start=(m == 0), stop=(m == NCH - 1))
                    for m in range(NCH):
                        S.mm(ps_bp[:, 0:512], we3[m][:, ic * 128:(ic + 1) * 128],
                             xn[m][:, tsl], start=(m == 0), stop=(m == NCH - 1))
                    S.act(sa[ic % 4][:], ps_a[:, 0:512], AF.Silu, bias=b1c[:, ic:ic + 1])
                    if not shared:
                        S.stt(bb[ic % 4][:], ps_bp[:, 0:512], b3c[:, ic:ic + 1],
                              ge_sb[:, tsl], ALU.add, ALU.mult)
                    else:
                        S.ts(bb[ic % 4][:], ps_bp[:, 0:512], b3c[:, ic:ic + 1], None, ALU.add)
                    S.tt(hh[ic][:], sa[ic % 4][:], bb[ic % 4][:], ALU.mult)
                for dm in range(NCH):
                    py = ps_y[dm % 2]
                    extra = (shared and sh == 0)
                    for ic in range(NCH):
                        S.mm(py[:, 0:512], we2[ic][:, dm * 128:(dm + 1) * 128],
                             hh[ic][:], start=(ic == 0), stop=(ic == NCH - 1 and not extra))
                    if extra:
                        S.mm(py[:, 0:512], b2a[:, dm * 128:(dm + 1) * 128],
                             gate_bf[:, tsl], start=False, stop=False)
                        S.mm(py[:, 0:512], bs2b[0:1, dm * 128:(dm + 1) * 128],
                             C.ones_row_bf[0:1, 0:512], start=False, stop=True)
                    yb = yy[(dm * 4 + t4) % 4]
                    S.copy("vector", yb[:], py[:, 0:512])
                    S.dma("qg", xout[dm * 128:(dm + 1) * 128, tsl], yb[:],
                          accum=True, wkey=f"xoacc{dm}")
        S.force_raw = True
        S.barrier()


def host_l0_ffn_inputs(ins):
    return {
        "gwT": np.ascontiguousarray(np.asarray(ins["L0_ffn_gate_w"]).T),
        "w1T": np.ascontiguousarray(np.asarray(ins["L0_ffn_w1"]).transpose(0, 2, 1)),
        "w3T": np.ascontiguousarray(np.asarray(ins["L0_ffn_w3"]).transpose(0, 2, 1)),
        "w2T": np.ascontiguousarray(np.asarray(ins["L0_ffn_w2"]).transpose(0, 2, 1)),
        "ws1T": np.ascontiguousarray(np.asarray(ins["L0_ffn_ws1"]).T),
        "ws3T": np.ascontiguousarray(np.asarray(ins["L0_ffn_ws3"]).T),
        "ws2T": np.ascontiguousarray(np.asarray(ins["L0_ffn_ws2"]).T),
        "bs1": np.asarray(ins["L0_ffn_bs1"]),
        "bs3": np.asarray(ins["L0_ffn_bs3"]),
        "bs2": np.asarray(ins["L0_ffn_bs2"]),
    }


def decl_l0_ffn_weights(nc):
    dts = {}
    for nm, sh in (("gwT", (1024, 8)), ("L0_ffn_gate_b", (8,)),
                   ("w1T", (8, 1024, 1024)), ("w3T", (8, 1024, 1024)), ("w2T", (8, 1024, 1024)),
                   ("L0_ffn_b1", (8, 1024)), ("L0_ffn_b3", (8, 1024)), ("L0_ffn_b2", (8, 1024)),
                   ("ws1T", (1024, 2048)), ("ws3T", (1024, 2048)), ("ws2T", (2048, 1024)),
                   ("bs1", (2048,)), ("bs3", (2048,)), ("bs2", (1024,))):
        dts[nm] = nc.dram_tensor(nm, list(sh), F32, kind="ExternalInput")
    return dts


def decl_consts(nc):
    import ml_dtypes
    dts = {}
    for k, v in host_constants().items():
        dt = BF16 if v.dtype == ml_dtypes.bfloat16 else F32
        dts[k] = nc.dram_tensor(k, list(v.shape), dt, kind="ExternalInput")
    return dts



def emit_l1_attn(nc, S, C, stack, xin, xout, W):
    NH = 16
    with contextlib.ExitStack() as st:
        xn = alloc_chunks(nc, st, "a1_xn", NCH, [128, T], BF16)
        emit_ln(nc, S, C, st, xin, xn, BF16)

        q_fm = alloc_chunks(nc, st, "a1_q", NCH, [128, T], BF16)
        out_pre = alloc_chunks(nc, st, "a1_op", NCH, [128, T], BF16)
        s1_sb = st.enter_context(nc.sbuf_tensor("a1_s1", [64, 1024], F32))
        s2_sb = st.enter_context(nc.sbuf_tensor("a1_s2", [64, 1024], F32))
        v1m = st.enter_context(nc.sbuf_tensor("a1_v1m", [1, 1024], F32))
        v2m = st.enter_context(nc.sbuf_tensor("a1_v2m", [1, 1024], F32))

        def load_wT(buf, key, add_ident):
            for r in range(NCH):
                S.dma("qg", buf[r][:], W[key].ap()[r * 128:(r + 1) * 128, :])
            if add_ident:
                for r in range(NCH):
                    S.tt(buf[r][:, r * 128:(r + 1) * 128], buf[r][:, r * 128:(r + 1) * 128],
                         C.ident_bf[:], ALU.add)

        def load_bias_col(sk, name, key):
            bc = sk.enter_context(nc.sbuf_tensor(name, [128, 8], F32))
            for c in range(NCH):
                S.dma("qs", bc[:, c], W[key].ap()[c * 128:(c + 1) * 128])
            return bc

        def load_bias_row(sk, name, key):
            br = sk.enter_context(nc.sbuf_tensor(name, [1, 1024], F32))
            bb = sk.enter_context(nc.sbuf_tensor(name + "b", [1, 1024], BF16))
            S.dma("qs", br[0:1, :], W[key].ap()[None, :])
            S.scopy(bb[:], br[:])
            return bb

        S.force_raw = False
        with contextlib.ExitStack() as qs_:
            wbuf = alloc_chunks(nc, qs_, "a1_w", NCH, [128, 1024], BF16)
            wvbuf = alloc_chunks(nc, qs_, "a1_wv", NCH, [128, 1024], BF16)
            bq_col = load_bias_col(qs_, "a1_bq", "L1_attn_bq")
            bk1_row = load_bias_row(qs_, "a1_bk1", "L1_attn_bk1")
            bk2_row = load_bias_row(qs_, "a1_bk2", "L1_attn_bk2")
            bv1_row = load_bias_row(qs_, "a1_bv1", "L1_attn_bv1")
            bv2_row = load_bias_row(qs_, "a1_bv2", "L1_attn_bv2")
            ps_p = [qs_.enter_context(nc.psum_tensor(f"a1_pp{i}", [128, 512], F32)) for i in range(2)]
            load_wT(wbuf, "WqT", True)
            for c in range(NCH):
                for t4 in range(4):
                    pp = ps_p[t4 % 2]
                    for m in range(NCH):
                        S.mm(pp[:, 0:512], wbuf[m][:, c * 128:(c + 1) * 128],
                             xn[m][:, t4 * 512:(t4 + 1) * 512],
                             start=(m == 0), stop=(m == NCH - 1))
                    S.ts(q_fm[c][:, t4 * 512:(t4 + 1) * 512], pp[:, 0:512],
                         bq_col[:, c:c + 1], None, ALU.add)

            kt = [qs_.enter_context(nc.sbuf_tensor(f"a1_kt{i}", [128, 384], BF16)) for i in range(2)]
            vt = [qs_.enter_context(nc.sbuf_tensor(f"a1_vt{i}", [128, 384], BF16)) for i in range(2)]
            ps_h = [qs_.enter_context(nc.psum_tensor(f"a1_psh{j}", [64, 64], F32)) for j in range(6)]
            mxn = qs_.enter_context(nc.sbuf_tensor("a1_mxn", [128, 8], F32))
            mxnb = qs_.enter_context(nc.sbuf_tensor("a1_mxnb", [128, 8], BF16))
            for c in range(NCH):
                S.red(mxn[:, c:c + 1], xn[c][:], ALU.add)
            S.ts(mxn[:], mxn[:], 1.0 / T, None, ALU.mult)
            S.scopy(mxnb[:], mxn[:])
            sets = [(0, 6), (6, 12), (12, 16)]
            for (wk, wv, bk_row, bv_row, s_sb, vm) in (
                    ("Wk1T", "Wv1T", bk1_row, bv1_row, s1_sb, v1m),
                    ("Wk2T", "Wv2T", bk2_row, bv2_row, s2_sb, v2m)):
                load_wT(wbuf, wk, True)
                load_wT(wvbuf, wv, True)
                for half in range(2):
                    pm = ps_p[half]
                    for r in range(NCH):
                        S.mm(pm[0:1, 0:512], mxnb[:, r:r + 1],
                             wvbuf[r][:, half * 512:(half + 1) * 512],
                             start=(r == 0), stop=False)
                    S.mm(pm[0:1, 0:512], C.ones_row_bf[0:1, 0:1],
                         bv_row[0:1, half * 512:(half + 1) * 512], start=False, stop=True)
                    S.copy("vector", vm[0:1, half * 512:(half + 1) * 512], pm[0:1, 0:512])
                for (h0, h1) in sets:
                    nh = h1 - h0
                    cw = nh * 64
                    for tc in range(16):
                        kk = kt[tc % 2]
                        vv = vt[tc % 2]
                        for (buf, brow, dst) in ((wbuf, bk_row, kk), (wvbuf, bv_row, vv)):
                            pp = ps_p[0 if dst is kk else 1]
                            for m in range(NCH):
                                S.mm(pp[:, 0:cw], xn[m][:, tc * 128:(tc + 1) * 128],
                                     buf[m][:, h0 * 64:h0 * 64 + cw],
                                     start=(m == 0), stop=False)
                            S.mm(pp[:, 0:cw], C.ones_row_bf[0:1, 0:128],
                                 brow[0:1, h0 * 64:h0 * 64 + cw], start=False, stop=True)
                            S.scopy(dst[:, 0:cw], pp[:, 0:cw])
                        for j in range(nh):
                            S.mm(ps_h[j][:], kk[:, j * 64:(j + 1) * 64],
                                 vv[:, j * 64:(j + 1) * 64],
                                 start=(tc == 0), stop=(tc == 15))
                    for j in range(nh):
                        S.ts(s_sb[:, (h0 + j) * 64:(h0 + j + 1) * 64], ps_h[j][:],
                             1.0 / T, None, ALU.mult)
            S.barrier()

        with contextlib.ExitStack() as cs_:
            gcols = cs_.enter_context(nc.sbuf_tensor("a1_gc", [64, 64], F32))
            grow = cs_.enter_context(nc.sbuf_tensor("a1_gr", [1, 64], F32))
            sc_bf = [cs_.enter_context(nc.sbuf_tensor(f"a1_sc{p}", [128, 64], BF16))
                     for p in range(NH // 2)]
            y2_bf = cs_.enter_context(nc.sbuf_tensor("a1_y2", [1, 1024], BF16))
            y2f = cs_.enter_context(nc.sbuf_tensor("a1_y2f", [1, 1024], F32))
            t1 = cs_.enter_context(nc.sbuf_tensor("a1_t1", [64, 64], F32))
            t2 = cs_.enter_context(nc.sbuf_tensor("a1_t2", [64, 64], F32))
            t3 = cs_.enter_context(nc.sbuf_tensor("a1_t3", [64, 64], F32))
            bcv = cs_.enter_context(nc.sbuf_tensor("a1_bcv", [64, 64], F32))
            ps_bc = cs_.enter_context(nc.psum_tensor("a1_pbc", [64, 64], F32))
            ps_r = cs_.enter_context(nc.psum_tensor("a1_pr", [1, 64], F32))
            ps_g = cs_.enter_context(nc.psum_tensor("a1_pg", [64, 64], F32))
            ps_y = [cs_.enter_context(nc.psum_tensor(f"a1_py{i}", [64, 512], F32)) for i in range(2)]

            S.dma("qs", grow[0:1, 0:16], W["L1_attn_g1"].ap()[None, :])
            S.dma("qs", grow[0:1, 16:32], W["L1_attn_g2"].ap()[None, :])
            S.dma("qs", grow[0:1, 32:48], W["L1_attn_g3"].ap()[None, :])
            S.dma("qs", grow[0:1, 48:64], W["L1_attn_g4"].ap()[None, :])
            S.mm(ps_g[:, 0:64], C.ones_row_f[0:1, 0:64], grow[0:1, 0:64])
            S.copy("vector", gcols[:], ps_g[:, 0:64])

            for h in range(NH):
                hs = slice(h * 64, (h + 1) * 64)
                S.mm(ps_bc[:], C.ones_row_f[0:1, 0:64], v2m[0:1, hs])
                S.copy("vector", bcv[:], ps_bc[:])
                S.tt(t1[:], s1_sb[:, hs], bcv[:], ALU.mult)
                S.ts(t1[:], t1[:], gcols[:, h:h + 1], None, ALU.mult)
                S.mm(ps_bc[:], C.ones_row_f[0:1, 0:64], v1m[0:1, hs])
                S.copy("vector", bcv[:], ps_bc[:])
                S.tt(t2[:], s2_sb[:, hs], bcv[:], ALU.mult)
                S.ts(t2[:], t2[:], gcols[:, 32 + h:32 + h + 1], None, ALU.mult)
                S.tt(t3[:], s1_sb[:, hs], s2_sb[:, hs], ALU.mult)
                S.mm(ps_r[:], C.ones_col_f[0:64, 0:1], t3[:])
                S.ts(y2f[0:1, hs], ps_r[:], gcols[0:1, 16 + h:16 + h + 1], None, ALU.mult)
                S.ts(t3[:], t3[:], gcols[:, 48 + h:48 + h + 1], None, ALU.mult)
                S.tt(t1[:], t1[:], t2[:], ALU.add)
                S.tt(sc_bf[h // 2][(h % 2) * 64:(h % 2) * 64 + 64, :], t1[:], t3[:], ALU.add)
            S.scopy(y2_bf[:], y2f[:])

            for h in range(NH):
                hs = slice(h * 64, (h + 1) * 64)
                for t4 in range(4):
                    py = ps_y[t4 % 2]
                    S.mm(py[:, 0:512], sc_bf[h // 2][(h % 2) * 64:(h % 2) * 64 + 64, :],
                         q_fm[h // 2][(h % 2) * 64:(h % 2) * 64 + 64, t4 * 512:(t4 + 1) * 512],
                         start=True, stop=False)
                    S.mm(py[:, 0:512], y2_bf[0:1, hs], C.ones_row_bf[0:1, 0:512],
                         start=False, stop=True)
                    S.copy("vector",
                           out_pre[h // 2][(h % 2) * 64:(h % 2) * 64 + 64, t4 * 512:(t4 + 1) * 512],
                           py[:, 0:512])
            S.barrier()

        with contextlib.ExitStack() as ls_:
            wbuf = alloc_chunks(nc, ls_, "a1_w2", NCH, [128, 1024], BF16)
            lng = load_bias_col(ls_, "a1_lng", "L1_attn_ln_g")
            lnb = load_bias_col(ls_, "a1_lnb", "L1_attn_ln_b")
            xn2 = alloc_chunks(nc, ls_, "a1_xn2", NCH, [128, T], BF16)
            sqq = [ls_.enter_context(nc.sbuf_tensor(f"a1_sq{i}", [128, 512], BF16)) for i in range(2)]
            tmp = [ls_.enter_context(nc.sbuf_tensor(f"a1_tm{i}", [128, 512], F32)) for i in range(2)]
            rowa = ls_.enter_context(nc.sbuf_tensor("a1_ra", [1, 512], F32))
            rowb = ls_.enter_context(nc.sbuf_tensor("a1_rb", [1, 512], F32))
            rowc = ls_.enter_context(nc.sbuf_tensor("a1_rc", [1, 512], F32))
            ps_s2 = ls_.enter_context(nc.psum_tensor("a1_ps2", [1, 512], F32))
            ps_q2 = ls_.enter_context(nc.psum_tensor("a1_pq2", [1, 512], F32))
            ps_a = ls_.enter_context(nc.psum_tensor("a1_pa2", [128, 512], F32))
            ps_b = ls_.enter_context(nc.psum_tensor("a1_pb2", [128, 512], F32))
            for t4 in range(4):
                tsl = slice(t4 * 512, (t4 + 1) * 512)
                for c in range(NCH):
                    s = sqq[c % 2]
                    S.act(s[:], out_pre[c][:, tsl], AF.Square)
                    S.mm(ps_s2[:], C.ones_col_bf[:], out_pre[c][:, tsl],
                         start=(c == 0), stop=(c == NCH - 1))
                    S.mm(ps_q2[:], C.ones_col_bf[:], s[:],
                         start=(c == 0), stop=(c == NCH - 1))
                S.ts(rowa[:], ps_s2[:], 1.0 / D, None, ALU.mult)
                S.ts(rowb[:], ps_q2[:], 1.0 / D, None, ALU.mult)
                S.tt(rowc[:], rowa[:], rowa[:], ALU.mult)
                S.tt(rowb[:], rowb[:], rowc[:], ALU.subtract)
                S.ts(rowb[:], rowb[:], 1e-5, None, ALU.add)
                S.rowpow(rowb[:], -0.5)
                S.stt(rowc[:], rowa[:], -1.0, rowb[:], ALU.mult, ALU.mult)
                S.mm(ps_a[:], C.ones_row_f[0:1, 0:128], rowb[:])
                S.mm(ps_b[:], C.ones_row_f[0:1, 0:128], rowc[:])
                for c in range(NCH):
                    tm = tmp[c % 2]
                    S.tt(tm[:], out_pre[c][:, tsl], ps_a[:], ALU.mult)
                    S.tt(tm[:], tm[:], ps_b[:], ALU.add)
                    S.act(xn2[c][:, tsl], tm[:], AF.Identity,
                          bias=lnb[:, c:c + 1], scale=lng[:, c:c + 1])
            S.barrier()

            bo_row = load_bias_row(ls_, "a1_bo", "L1_attn_bo")
            load_wT(wbuf, "WoT_1", False)
            xt = [ls_.enter_context(nc.sbuf_tensor(f"a1_xt{i}", [128, 512], F32)) for i in range(2)]
            xo = [ls_.enter_context(nc.sbuf_tensor(f"a1_xo{i}", [128, 512], F32)) for i in range(2)]
            for dm in range(NCH):
                for t4 in range(4):
                    pso = ps_a if (dm * 4 + t4) % 2 == 0 else ps_b
                    for ic in range(NCH):
                        S.mm(pso[:], wbuf[ic][:, dm * 128:(dm + 1) * 128],
                             xn2[ic][:, t4 * 512:(t4 + 1) * 512], start=(ic == 0), stop=False)
                    S.mm(pso[:], bo_row[0:1, dm * 128:(dm + 1) * 128],
                         C.ones_row_bf[0:1, 0:512], start=False, stop=True)
                    i2 = (dm * 4 + t4) % 2
                    S.dma("qs", xt[i2][:], xin[dm * 128:(dm + 1) * 128, t4 * 512:(t4 + 1) * 512])
                    S.tt(xo[i2][:], pso[:], xt[i2][:], ALU.add)
                    S.dma("qs", xout[dm * 128:(dm + 1) * 128, t4 * 512:(t4 + 1) * 512], xo[i2][:])
            S.force_raw = True
            S.barrier()


def host_l1_attn_inputs(ins):
    return {
        "WqT": np.ascontiguousarray(np.asarray(ins["L1_attn_Wq"]).T),
        "Wk1T": np.ascontiguousarray(np.asarray(ins["L1_attn_Wk1"]).T),
        "Wk2T": np.ascontiguousarray(np.asarray(ins["L1_attn_Wk2"]).T),
        "Wv1T": np.ascontiguousarray(np.asarray(ins["L1_attn_Wv1"]).T),
        "Wv2T": np.ascontiguousarray(np.asarray(ins["L1_attn_Wv2"]).T),
        "WoT_1": np.ascontiguousarray(np.asarray(ins["L1_attn_Wo"]).T),
    }


def decl_l1_attn_weights(nc):
    dts = {}
    for nm, sh in (("WqT", (1024, 1024)), ("Wk1T", (1024, 1024)), ("Wk2T", (1024, 1024)),
                   ("Wv1T", (1024, 1024)), ("Wv2T", (1024, 1024)), ("WoT_1", (1024, 1024)),
                   ("L1_attn_bq", (1024,)), ("L1_attn_bk1", (1024,)), ("L1_attn_bk2", (1024,)),
                   ("L1_attn_bv1", (1024,)), ("L1_attn_bv2", (1024,)), ("L1_attn_bo", (1024,)),
                   ("L1_attn_g1", (16,)), ("L1_attn_g2", (16,)), ("L1_attn_g3", (16,)),
                   ("L1_attn_g4", (16,)), ("L1_attn_ln_g", (1024,)), ("L1_attn_ln_b", (1024,))):
        dts[nm] = nc.dram_tensor(nm, list(sh), F32, kind="ExternalInput")
    return dts



def emit_l1_ffn(nc, S, C, stack, xin, xout, W):
    NHH = 8
    HID = 2730
    NIC = 22
    u_dram = nc.dram_tensor("f1_u_scr", [1024, T], F32).ap()
    with contextlib.ExitStack() as st:
        emit_ln(nc, S, C, st, xin, None, F32, out_dram=u_dram)

        rot2 = st.enter_context(nc.sbuf_tensor("f1_rot2", [128, 128], BF16))
        S.dma("qs", rot2[:], W["c_rot2"].ap())
        tabs = {nm: st.enter_context(nc.sbuf_tensor(f"f1_{nm}", [128, T], BF16))
                for nm in ("csq2", "snq2", "csk2", "snk2")}
        for nm in ("csq2", "snq2", "csk2", "snk2"):
            S.dma("qg", tabs[nm][:], W[nm].ap())

        S.force_raw = False
        for l in range(2):
          with contextlib.ExitStack() as lst:
            A = alloc_chunks(nc, lst, f"f1_A{l}", 24, [128, T], BF16)
            qk_fm = A[0:8]
            v_tok = [A[8 + tc // 2] for tc in range(16)]
            vcol = lambda tc, c0, c1: v_tok[tc][:, (tc % 2) * 1024 + c0:(tc % 2) * 1024 + c1]
            o_fm = A[16:20]
            hsh = A[16:24]
            with contextlib.ExitStack() as s1:
                P = [s1.enter_context(nc.psum_tensor(f"f1_P{i}", [128, 512], F32))
                     for i in range(8)]
                for c in range(NCH):
                    if c < 4:
                        S.dma("qg", hsh[c][:], u_dram[c * 128:(c + 1) * 128, :])
                    else:
                        S.memset(hsh[c][:, 0:1], 0.0)
                        S.dma("qg", hsh[c][:, 1:T], u_dram[c * 128:(c + 1) * 128, 0:T - 1])
                wq = alloc_chunks(nc, s1, "f1_wq", NCH, [128, 1544], BF16)
                for r in range(NCH):
                    S.dma("qg", wq[r][:], W["wqkv_aug"].ap()[l, r * 128:(r + 1) * 128, :])
                vone = s1.enter_context(nc.sbuf_tensor("f1_vone", [1, 520], F32))
                voneb = s1.enter_context(nc.sbuf_tensor("f1_voneb", [1, 520], BF16))
                S.dma("qs", vone[:], W["vones_row"].ap())
                S.scopy(voneb[:], vone[:])
                for c in range(NCH):
                    for t4 in range(4):
                        pp = P[t4 % 2]
                        for m in range(NCH):
                            S.mm(pp[:, 0:512], wq[m][:, c * 128:(c + 1) * 128],
                                 hsh[m][:, t4 * 512:(t4 + 1) * 512],
                                 start=(m == 0), stop=(m == NCH - 1))
                        S.copy("vector", qk_fm[c][:, t4 * 512:(t4 + 1) * 512], pp[:, 0:512])
                for tc in range(16):
                    pp = P[tc % 2]
                    for half in range(2):
                        for m in range(NCH):
                            S.mm(pp[:, 0:260], hsh[m][:, tc * 128:(tc + 1) * 128],
                                 wq[m][:, 1024 + half * 260:1024 + (half + 1) * 260],
                                 start=(m == 0), stop=False)
                        S.mm(pp[:, 0:260], C.ones_row_bf[0:1, 0:128],
                             voneb[0:1, half * 260:(half + 1) * 260], start=False, stop=True)
                        S.copy("vector", vcol(tc, half * 260, (half + 1) * 260), pp[:, 0:260])

                qs2 = s1.enter_context(nc.sbuf_tensor("f1_qs2", [128, 1], F32))
                ks2 = s1.enter_context(nc.sbuf_tensor("f1_ks2", [128, 1], F32))
                for b0 in (0, 64):
                    S.dma("qs", qs2[b0:b0 + 64, 0], W["L1_ffn_q_scale"].ap()[l, :])
                    S.dma("qs", ks2[b0:b0 + 64, 0], W["L1_ffn_k_scale"].ap()[l, :])
                S.ts(qs2[:], qs2[:], 8.0, None, ALU.mult)
                sq128 = [s1.enter_context(nc.sbuf_tensor(f"f1_sq{i}", [128, 512], BF16)) for i in range(2)]
                nrA = s1.enter_context(nc.sbuf_tensor("f1_nrA", [1, T], F32))
                nrB = s1.enter_context(nc.sbuf_tensor("f1_nrB", [1, T], F32))
                bcv = [s1.enter_context(nc.sbuf_tensor(f"f1_bcv{i}", [128, 512], F32)) for i in range(2)]
                tt1 = [s1.enter_context(nc.sbuf_tensor(f"f1_tt1{i}", [128, 512], BF16)) for i in range(2)]
                tt2 = [s1.enter_context(nc.sbuf_tensor(f"f1_tt2{i}", [128, 512], BF16)) for i in range(2)]
                for c in (0, 4, 1, 5, 2, 6, 3, 7):
                    isq = c < 4
                    scol = qs2 if isq else ks2
                    cst = tabs["csq2" if isq else "csk2"]
                    snt = tabs["snq2" if isq else "snk2"]
                    ch = qk_fm[c]
                    for t4 in range(4):
                        S.tt(sq128[t4 % 2][:], ch[:, t4 * 512:(t4 + 1) * 512],
                             ch[:, t4 * 512:(t4 + 1) * 512], ALU.mult, eng="gpsimd")
                        for (b0, nr) in ((0, nrA), (64, nrB)):
                            S.mm(P[2][0:1, 0:512], C.ones_col_bf[b0:b0 + 64, :],
                                 sq128[t4 % 2][b0:b0 + 64, :])
                            S.copy("vector", nr[:, t4 * 512:(t4 + 1) * 512], P[2][0:1, 0:512])
                    for nr in (nrA, nrB):
                        S.ts(nr[:], nr[:], 1e-24, None, ALU.max)
                        S.rowpow(nr[:], -0.5)
                    for t4 in range(4):
                        tsl = slice(t4 * 512, (t4 + 1) * 512)
                        S.mm(P[3][0:64, 0:512], C.ones_row_f[0:1, 0:64], nrA[0:1, tsl])
                        S.mm(P[3][64:128, 0:512], C.ones_row_f[0:1, 0:64], nrB[0:1, tsl])
                        ib = t4 % 2
                        S.copy("vector", bcv[ib][:], P[3][:, 0:512])
                        S.stt(ch[:, tsl], ch[:, tsl], scol[:], bcv[ib][:], ALU.mult, ALU.mult)
                        S.mm(P[4 + ib][:, 0:512], rot2[:], ch[:, tsl])
                        S.tt(tt1[ib][:], ch[:, tsl], cst[:, tsl], ALU.mult)
                        S.tt(tt2[ib][:], P[4 + ib][:, 0:512], snt[:, tsl], ALU.mult)
                        S.tt(ch[:, tsl], tt1[ib][:], tt2[ib][:], ALU.add)

                q_aug = s1.enter_context(nc.sbuf_tensor("f1_qa", [65, T], BF16))
                k_aug = s1.enter_context(nc.sbuf_tensor("f1_ka", [65, T], BF16))
                mnb = [s1.enter_context(nc.sbuf_tensor(f"f1_mnb{i}", [128, 1], BF16))
                       for i in range(8)]
                mcols = [s1.enter_context(nc.sbuf_tensor(f"f1_mcol{i}", [128, 1], F32))
                         for i in range(4)]
                mtl = [s1.enter_context(nc.sbuf_tensor(f"f1_mtl{i}", [128, 1], F32))
                       for i in range(4)]
                prb = [s1.enter_context(nc.sbuf_tensor(f"f1_prb{i}", [128, 512], BF16)) for i in range(3)]
                rrec = s1.enter_context(nc.sbuf_tensor("f1_rr", [1, 512], F32))
                bcs = s1.enter_context(nc.sbuf_tensor("f1_bcs", [64, 512], F32))
                ps_s = [P[0], P[1], P[2]]
                ps_avs = [P[6], P[7]]
                for h in range(NHH):
                    qch, b0 = qk_fm[h // 2], (h % 2) * 64
                    kch = qk_fm[4 + h // 2]
                    S.copy("gpsimd", q_aug[0:64, :], qch[b0:b0 + 64, :])
                    S.copy("gpsimd", k_aug[0:64, :], kch[b0:b0 + 64, :])
                    S.memset(k_aug[64:65, :], 1.0)
                    for j in range(4):
                        ps_av = ps_avs[j % 2]
                        qsl = slice(j * 512, (j + 1) * 512)
                        for qc4 in range(4):
                            qq = q_aug[0:64, j * 512 + qc4 * 128: j * 512 + (qc4 + 1) * 128]
                            mcol = mcols[qc4]
                            pss1 = ps_s[qc4 % 3]
                            if j == 0:
                                kc0, kw = 0, (qc4 + 1) * 128
                                S.mm(pss1[:, 0:kw], qq, k_aug[0:64, kc0:kc0 + kw])
                                S.tt(pss1[:, qc4 * 128:kw], pss1[:, qc4 * 128:kw],
                                     C.t_up[:], ALU.add)
                                S.red(mcol[:], pss1[:, 0:kw], ALU.max)
                            else:
                                kc0 = (j - 1) * 512 + qc4 * 128
                                S.mm(pss1[:, 0:512], qq, k_aug[0:64, kc0:kc0 + 512])
                                S.tt(pss1[:, 0:128], pss1[:, 0:128], C.t_lo[:], ALU.add)
                                S.mm(P[3][:, qc4 * 128:(qc4 + 1) * 128], qq,
                                     k_aug[0:64, kc0 + 512:kc0 + 640])
                                S.tt(P[3][:, qc4 * 128:(qc4 + 1) * 128],
                                     P[3][:, qc4 * 128:(qc4 + 1) * 128], C.t_up[:], ALU.add)
                                S.red(mcol[:], pss1[:, 0:512], ALU.max)
                                S.red(mtl[qc4][:], P[3][:, qc4 * 128:(qc4 + 1) * 128], ALU.max)
                                S.tt(mcol[:], mcol[:], mtl[qc4][:], ALU.max)
                            mb = mnb[(j % 2) * 4 + qc4]
                            S.ts(mb[:], mcol[:], -1.0, None, ALU.mult)
                            S.dma("qs", q_aug[64:65, j * 512 + qc4 * 128: j * 512 + (qc4 + 1) * 128],
                                  mb[:])
                        if j == 0:
                            kslist = [(ks, ks * 128, ks * 128, 512 - ks * 128, "lo")
                                      for ks in range(4)]
                        else:
                            kslist = []
                            for ks in range(8):
                                kc = (j - 1) * 512 + ks * 128
                                if ks <= 3:
                                    kslist.append((kc // 128, kc, 0, (ks + 1) * 128, "up"))
                                else:
                                    kslist.append((kc // 128, kc, (ks - 4) * 128, 512 - (ks - 4) * 128, "lo"))
                        nck = len(kslist)

                        def emit_score2(i):
                            vc, kc, n0, nw, mty = kslist[i]
                            pss = ps_s[i % 3]
                            S.mm(pss[:, n0:n0 + nw], k_aug[:, kc:kc + 128],
                                 q_aug[:, j * 512 + n0: j * 512 + n0 + nw])
                            if mty == "up":
                                S.tt(pss[:, n0 + nw - 128:n0 + nw], pss[:, n0 + nw - 128:n0 + nw],
                                     C.t_up[:], ALU.add)
                            else:
                                S.tt(pss[:, n0:n0 + 128], pss[:, n0:n0 + 128], C.t_lo[:], ALU.add)
                            pr = prb[i % 3]
                            if n0 > 0:
                                S.memset(pr[:, 0:n0], 0.0)
                            if n0 + nw < 512:
                                S.memset(pr[:, n0 + nw:512], 0.0)
                            S.act(pr[:, n0:n0 + nw], pss[:, n0:n0 + nw], AF.Exp)

                        def emit_av2(i):
                            vc = kslist[i][0]
                            S.mm(ps_av[0:65, 0:512], vcol(vc, h * 65, (h + 1) * 65), prb[i % 3][:],
                                 start=(i == 0), stop=(i == nck - 1))

                        for i in range(nck):
                            emit_score2(i)
                            if i >= 2:
                                emit_av2(i - 2)
                        emit_av2(nck - 2)
                        emit_av2(nck - 1)
                        S.copy("vector", rrec[:], ps_av[64:65, 0:512])
                        S.rowpow(rrec[:], -1.0)
                        S.mm(P[4][0:64, 0:512], C.ones_row_f[0:1, 0:64], rrec[0:1, :])
                        S.copy("vector", bcs[:], P[4][0:64, 0:512])
                        S.tt(o_fm[h // 2][b0:b0 + 64, qsl], ps_av[0:64, 0:512], bcs[:], ALU.mult)

                wo = alloc_chunks(nc, s1, "f1_wo", 4, [128, 1024], BF16)
                for r in range(4):
                    S.dma("qg", wo[r][:], W["L1_ffn_wo"].ap()[l, r * 128:(r + 1) * 128, :])
                ut = [s1.enter_context(nc.sbuf_tensor(f"f1_ut{i}", [128, 512], F32)) for i in range(2)]
                uo = [s1.enter_context(nc.sbuf_tensor(f"f1_uo{i}", [128, 512], F32)) for i in range(2)]
                for dm in range(NCH):
                    for t4 in range(4):
                        i2 = (dm * 4 + t4) % 2
                        py = P[i2]
                        tsl = slice(t4 * 512, (t4 + 1) * 512)
                        for ic in range(4):
                            S.mm(py[:, 0:512], wo[ic][:, dm * 128:(dm + 1) * 128],
                                 o_fm[ic][:, tsl], start=(ic == 0), stop=(ic == 3))
                        S.dma("qs", ut[i2][:], u_dram[dm * 128:(dm + 1) * 128, tsl])
                        S.tt(uo[i2][:], py[:, 0:512], ut[i2][:], ALU.add)
                        S.dma("qs", u_dram[dm * 128:(dm + 1) * 128, tsl], uo[i2][:])
                S.barrier()

            with contextlib.ExitStack() as s2:
                P = [s2.enter_context(nc.psum_tensor(f"f1_Q{i}", [128, 512], F32))
                     for i in range(8)]
                xnf = alloc_chunks(nc, s2, f"f1_xnf{l}", NCH, [128, T], BF16)
                ush = A[8:16]
                for c in range(NCH):
                    if c < 4:
                        S.dma("qg", ush[c][:], u_dram[c * 128:(c + 1) * 128, :])
                    else:
                        S.memset(ush[c][:, 0:1], 0.0)
                        S.dma("qg", ush[c][:, 1:T], u_dram[c * 128:(c + 1) * 128, 0:T - 1])
                lng = s2.enter_context(nc.sbuf_tensor("f1_lng", [128, 8], F32))
                lnb = s2.enter_context(nc.sbuf_tensor("f1_lnb", [128, 8], F32))
                for c in range(NCH):
                    S.dma("qs", lng[:, c], W["L1_ffn_ln_g"].ap()[l, c * 128:(c + 1) * 128])
                    S.dma("qs", lnb[:, c], W["L1_ffn_ln_b"].ap()[l, c * 128:(c + 1) * 128])
                sqq = [s2.enter_context(nc.sbuf_tensor(f"f1_sqq{i}", [128, 512], BF16)) for i in range(2)]
                tmp = [s2.enter_context(nc.sbuf_tensor(f"f1_tmq{i}", [128, 512], F32)) for i in range(2)]
                rows = [[s2.enter_context(nc.sbuf_tensor(f"f1_r{nm}{i}", [1, 512], F32))
                         for i in range(2)] for nm in "abc"]
                for t4 in range(4):
                    rowa, rowb, rowc = rows[0][t4 % 2], rows[1][t4 % 2], rows[2][t4 % 2]
                    ps_s2, ps_q2 = P[4 + t4 % 2], P[6 + t4 % 2]
                    ps_a, ps_b2 = P[t4 % 2], P[2 + t4 % 2]
                    tsl = slice(t4 * 512, (t4 + 1) * 512)
                    for c in range(NCH):
                        s_ = sqq[c % 2]
                        S.tt(s_[:], ush[c][:, tsl], ush[c][:, tsl], ALU.mult, eng="gpsimd")
                        S.mm(ps_s2[0:1, 0:512], C.ones_col_bf[:], ush[c][:, tsl],
                             start=(c == 0), stop=(c == NCH - 1))
                        S.mm(ps_q2[0:1, 0:512], C.ones_col_bf[:], s_[:],
                             start=(c == 0), stop=(c == NCH - 1))
                    S.ts(rowa[:], ps_s2[0:1, 0:512], 1.0 / D, None, ALU.mult)
                    S.ts(rowb[:], ps_q2[0:1, 0:512], 1.0 / D, None, ALU.mult)
                    S.tt(rowc[:], rowa[:], rowa[:], ALU.mult)
                    S.tt(rowb[:], rowb[:], rowc[:], ALU.subtract)
                    S.ts(rowb[:], rowb[:], 1e-5, None, ALU.add)
                    S.rowpow(rowb[:], -0.5)
                    S.stt(rowc[:], rowa[:], -1.0, rowb[:], ALU.mult, ALU.mult)
                    S.mm(ps_a[:, 0:512], C.ones_row_f[0:1, 0:128], rowb[:])
                    S.mm(ps_b2[:, 0:512], C.ones_row_f[0:1, 0:128], rowc[:])
                    for c in range(NCH):
                        tm = tmp[c % 2]
                        S.tt(tm[:], ush[c][:, tsl], ps_a[:, 0:512], ALU.mult)
                        S.tt(tm[:], tm[:], ps_b2[:, 0:512], ALU.add)
                        S.act(xnf[c][:, tsl], tm[:], AF.Identity,
                              bias=lnb[:, c:c + 1], scale=lng[:, c:c + 1])

                hh = A[0:22]
                gsc = contextlib.ExitStack()
                w1ag = [alloc_chunks(nc, gsc, f"f1_w1a{i}", NCH, [128, 512], BF16) for i in range(2)]
                w1gg = [alloc_chunks(nc, gsc, f"f1_w1g{i}", NCH, [128, 512], BF16) for i in range(2)]
                b1aA = gsc.enter_context(nc.sbuf_tensor("f1_b1aA", [128, NIC], F32))
                b1gA = gsc.enter_context(nc.sbuf_tensor("f1_b1gA", [128, NIC], F32))
                for ic in range(NIC):
                    cw = min(128, HID - ic * 128)
                    S.dma("qs", b1aA[0:cw, ic], W["L1_ffn_b1"].ap()[l, ic * 128:ic * 128 + cw])
                    S.dma("qs", b1gA[0:cw, ic], W["L1_ffn_b1"].ap()[l, HID + ic * 128:HID + ic * 128 + cw])
                aa = [gsc.enter_context(nc.sbuf_tensor(f"f1_aa{i}", [128, 512], BF16)) for i in range(2)]
                gg = [gsc.enter_context(nc.sbuf_tensor(f"f1_gg{i}", [128, 512], BF16)) for i in range(2)]
                NG = (NIC + 3) // 4
                for icg in range(NG):
                    gc0 = icg * 512
                    gw = min(512, HID - gc0)
                    wa, wg = w1ag[icg % 2], w1gg[icg % 2]
                    for r in range(NCH):
                        S.dma("qg", wa[r][:, 0:gw],
                              W["L1_ffn_w1"].ap()[l, r * 128:(r + 1) * 128, gc0:gc0 + gw])
                        S.dma("qg", wg[r][:, 0:gw],
                              W["L1_ffn_w1"].ap()[l, r * 128:(r + 1) * 128, HID + gc0:HID + gc0 + gw])
                    for ici in range((gw + 127) // 128):
                        ic = icg * 4 + ici
                        cw = min(128, HID - ic * 128)
                        for t4 in range(4):
                            tsl = slice(t4 * 512, (t4 + 1) * 512)
                            pa, pg = P[(t4 % 2) * 2], P[(t4 % 2) * 2 + 1]
                            for m in range(NCH):
                                S.mm(pa[0:cw, 0:512], wa[m][:, ici * 128:ici * 128 + cw],
                                     xnf[m][:, tsl], start=(m == 0), stop=(m == NCH - 1))
                            for m in range(NCH):
                                S.mm(pg[0:cw, 0:512], wg[m][:, ici * 128:ici * 128 + cw],
                                     xnf[m][:, tsl], start=(m == 0), stop=(m == NCH - 1))
                            S.ts(aa[t4 % 2][0:cw, :], pa[0:cw, 0:512],
                                 b1aA[0:cw, ic:ic + 1], None, ALU.add)
                            S.act(gg[t4 % 2][0:cw, :], pg[0:cw, 0:512], AF.Gelu,
                                  bias=b1gA[0:cw, ic:ic + 1])
                            S.tt(hh[ic][0:cw, tsl], aa[t4 % 2][0:cw, :], gg[t4 % 2][0:cw, :],
                                 ALU.mult)

                S.barrier()
                gsc.close()
                w2m = [s2.enter_context(nc.sbuf_tensor(f"f1_w2m{i}", [128, 2688], BF16)) for i in range(2)]
                w2t = [s2.enter_context(nc.sbuf_tensor(f"f1_w2t{i}", [42, 128], BF16)) for i in range(2)]
                b2r = s2.enter_context(nc.sbuf_tensor("f1_b2r", [1, 1024], F32))
                b2b = s2.enter_context(nc.sbuf_tensor("f1_b2b", [1, 1024], BF16))
                S.dma("qs", b2r[0:1, :], W["L1_ffn_b2"].ap()[l:l + 1, :])
                S.scopy(b2b[:], b2r[:])
                ut2 = [s2.enter_context(nc.sbuf_tensor(f"f1_ut2{i}", [128, 512], F32)) for i in range(2)]
                uo2 = [s2.enter_context(nc.sbuf_tensor(f"f1_uo2{i}", [128, 512], F32)) for i in range(2)]
                for dm in range(NCH):
                    wm, wt = w2m[dm % 2], w2t[dm % 2]
                    S.dma("qg", wm[:].rearrange("p (i c) -> p i c", c=128),
                          W["L1_ffn_w2"].ap()[l, 0:2688, dm * 128:(dm + 1) * 128]
                          .rearrange("(i p) c -> p i c", p=128))
                    S.dma("qg", wt[:], W["L1_ffn_w2"].ap()[l, 2688:2730, dm * 128:(dm + 1) * 128])
                    for t4 in range(4):
                        i2 = (dm * 4 + t4) % 2
                        py = P[4 + i2]
                        tsl = slice(t4 * 512, (t4 + 1) * 512)
                        for ic in range(21):
                            S.mm(py[:, 0:512], wm[:, ic * 128:(ic + 1) * 128], hh[ic][:, tsl],
                                 start=(ic == 0), stop=False)
                        S.mm(py[:, 0:512], wt[0:42, :], hh[21][0:42, tsl],
                             start=False, stop=False)
                        S.mm(py[:, 0:512], b2b[0:1, dm * 128:(dm + 1) * 128],
                             C.ones_row_bf[0:1, 0:512], start=False, stop=True)
                        S.dma("qs", ut2[i2][:], u_dram[dm * 128:(dm + 1) * 128, tsl])
                        S.tt(uo2[i2][:], py[:, 0:512], ut2[i2][:], ALU.add)
                        S.dma("qs", u_dram[dm * 128:(dm + 1) * 128, tsl], uo2[i2][:])
                S.barrier()

        S.force_raw = True
        with contextlib.ExitStack() as os_:
            xt = [os_.enter_context(nc.sbuf_tensor(f"f1_xt{i}", [128, T], F32)) for i in range(2)]
            xu = [os_.enter_context(nc.sbuf_tensor(f"f1_xu{i}", [128, T], F32)) for i in range(2)]
            xo = [os_.enter_context(nc.sbuf_tensor(f"f1_xo{i}", [128, T], F32)) for i in range(2)]
            for c in range(NCH):
                i2 = c % 2
                S.dma("qs", xt[i2][:], xin[c * 128:(c + 1) * 128, :])
                S.dma("qs", xu[i2][:], u_dram[c * 128:(c + 1) * 128, :])
                S.tt(xo[i2][:], xu[i2][:], xt[i2][:], ALU.add)
                S.dma("qs", xout[c * 128:(c + 1) * 128, :], xo[i2][:])
            S.barrier()


def host_l1_ffn_inputs(ins):
    wqkv = np.asarray(ins["L1_ffn_wqkv"])
    aug = np.zeros((2, 1024, 1544), np.float32)
    aug[:, :, 0:1024] = wqkv[:, :, 0:1024]
    for h in range(8):
        aug[:, :, 1024 + h * 65:1024 + h * 65 + 64] = wqkv[:, :, 1024 + h * 64:1024 + (h + 1) * 64]
    vones = np.zeros((1, 520), np.float32)
    for h in range(8):
        vones[0, h * 65 + 64] = 1.0
    DIM_HEAD, WINDOW, n = 64, 512, 2048
    half_idx = np.arange(0, DIM_HEAD, 2, dtype=np.float32)
    inv_freq = 1.0 / 10000.0 ** (half_idx / DIM_HEAD)
    t = np.arange(n, dtype=np.float32)
    freqs = t[:, None] * inv_freq[None, :]
    freqs = np.concatenate([freqs, freqs], axis=-1)
    base = (half_idx + 0.4 * DIM_HEAD) / (1.4 * DIM_HEAD)
    power = (t - n // 2) / float(WINDOW // 2)
    scale = base[None, :] ** power[:, None]
    scale = np.concatenate([scale, scale], axis=-1)
    cos, sin = np.cos(freqs), np.sin(freqs)
    csq = (cos * scale).T.astype(np.float32)
    snq = (sin * scale).T.astype(np.float32)
    csk = (cos / scale).T.astype(np.float32)
    snk = (sin / scale).T.astype(np.float32)
    stack2 = lambda a: np.ascontiguousarray(np.concatenate([a, a], axis=0))
    import ml_dtypes
    R = np.zeros((64, 64), np.float32)
    for i in range(32):
        R[i, 32 + i] = -1.0
        R[32 + i, i] = 1.0
    RT = R.T
    rot2 = np.zeros((128, 128), np.float32)
    rot2[0:64, 0:64] = RT
    rot2[64:128, 64:128] = RT
    return {
        "wqkv_aug": aug, "vones_row": vones,
        "csq2": stack2(csq), "snq2": stack2(snq), "csk2": stack2(csk), "snk2": stack2(snk),
        "c_rot2": rot2.astype(ml_dtypes.bfloat16),
    }


def decl_l1_ffn_weights(nc):
    dts = {}
    for nm, sh, dt in (("wqkv_aug", (2, 1024, 1544), F32), ("vones_row", (1, 520), F32),
                       ("csq2", (128, 2048), F32), ("snq2", (128, 2048), F32),
                       ("csk2", (128, 2048), F32), ("snk2", (128, 2048), F32),
                       ("c_rot2", (128, 128), BF16),
                       ("L1_ffn_q_scale", (2, 64), F32), ("L1_ffn_k_scale", (2, 64), F32),
                       ("L1_ffn_wo", (2, 512, 1024), F32),
                       ("L1_ffn_ln_g", (2, 1024), F32), ("L1_ffn_ln_b", (2, 1024), F32),
                       ("L1_ffn_w1", (2, 1024, 5460), F32), ("L1_ffn_b1", (2, 5460), F32),
                       ("L1_ffn_w2", (2, 2730, 1024), F32), ("L1_ffn_b2", (2, 1024), F32)):
        dts[nm] = nc.dram_tensor(nm, list(sh), dt, kind="ExternalInput")
    return dts



from concourse.bass_utils import run_bass_kernel_spmd

FUSED = True

_PASSTHRU = [
    "L0_attn_bq", "L0_attn_bk", "L0_attn_bv", "L0_attn_bo",
    "L0_attn_Wq", "L0_attn_Wk", "L0_attn_Wv",
    "L0_ffn_gate_b", "L0_ffn_b1", "L0_ffn_b3", "L0_ffn_b2",
    "L1_attn_bq", "L1_attn_bk1", "L1_attn_bk2", "L1_attn_bv1", "L1_attn_bv2",
    "L1_attn_bo", "L1_attn_g1", "L1_attn_g2", "L1_attn_g3", "L1_attn_g4",
    "L1_attn_ln_g", "L1_attn_ln_b",
    "L1_ffn_q_scale", "L1_ffn_k_scale", "L1_ffn_wo", "L1_ffn_ln_g", "L1_ffn_ln_b",
    "L1_ffn_w1", "L1_ffn_b1", "L1_ffn_w2", "L1_ffn_b2",
]

_BLOCKS = ("l0a", "l0f", "l1a", "l1f")

_DECL = {"l0a": decl_l0_attn_weights, "l0f": decl_l0_ffn_weights,
         "l1a": decl_l1_attn_weights, "l1f": decl_l1_ffn_weights}
_EMIT = {"l0a": emit_l0_attn, "l0f": emit_l0_ffn, "l1a": emit_l1_attn, "l1f": emit_l1_ffn}

_KEYS = {
    "l0a": ["L0_attn_Wq", "L0_attn_Wk", "L0_attn_Wv", "WoT", "L0_attn_bq", "L0_attn_bk",
            "L0_attn_bv", "L0_attn_bo", "A_q", "A_k", "A_v", "vb_ones", "cfl"],
    "l0f": ["gwT", "L0_ffn_gate_b", "w1T", "w3T", "w2T", "L0_ffn_b1", "L0_ffn_b3",
            "L0_ffn_b2", "ws1T", "ws3T", "ws2T", "bs1", "bs3", "bs2"],
    "l1a": ["WqT", "Wk1T", "Wk2T", "Wv1T", "Wv2T", "WoT_1", "L1_attn_bq", "L1_attn_bk1",
            "L1_attn_bk2", "L1_attn_bv1", "L1_attn_bv2", "L1_attn_bo", "L1_attn_g1",
            "L1_attn_g2", "L1_attn_g3", "L1_attn_g4", "L1_attn_ln_g", "L1_attn_ln_b"],
    "l1f": ["wqkv_aug", "vones_row", "csq2", "snq2", "csk2", "snk2", "c_rot2",
            "L1_ffn_q_scale", "L1_ffn_k_scale", "L1_ffn_wo", "L1_ffn_ln_g", "L1_ffn_ln_b",
            "L1_ffn_w1", "L1_ffn_b1", "L1_ffn_w2", "L1_ffn_b2"],
}


def _build_block(which):
    nc = bass.Bass("TRN2", target_bir_lowering=False)
    xin = nc.dram_tensor("xin", [1024, T], F32, kind="ExternalInput")
    xout = nc.dram_tensor("xout", [1024, T], F32, kind="ExternalOutput")
    dts = decl_consts(nc)
    W = _DECL[which](nc)
    with contextlib.ExitStack() as stack:
        S = Sync(nc, stack)
        C = Consts(nc, S, stack, dts)
        _EMIT[which](nc, S, C, stack, xin.ap(), xout.ap(), W)
        S.barrier()
    return nc


def _build_fused():
    nc = bass.Bass("TRN2", target_bir_lowering=False)
    xin = nc.dram_tensor("xin", [1024, T], F32, kind="ExternalInput")
    xout = nc.dram_tensor("xout", [1024, T], F32, kind="ExternalOutput")
    x1 = nc.dram_tensor("x_mid1", [1024, T], F32)
    x2 = nc.dram_tensor("x_mid2", [1024, T], F32)
    x3 = nc.dram_tensor("x_mid3", [1024, T], F32)
    dts = decl_consts(nc)
    W = {}
    for which in _BLOCKS:
        W.update(_DECL[which](nc))
    with contextlib.ExitStack() as stack:
        S = Sync(nc, stack)
        C = Consts(nc, S, stack, dts)
        emit_l0_attn(nc, S, C, stack, xin.ap(), x1.ap(), W)
        emit_l0_ffn(nc, S, C, stack, x1.ap(), x2.ap(), W)
        emit_l1_attn(nc, S, C, stack, x2.ap(), x3.ap(), W)
        emit_l1_ffn(nc, S, C, stack, x3.ap(), xout.ap(), W)
        S.barrier()
    return nc


def _host_inputs(inputs):
    host = {}
    host.update(host_constants())
    host.update(host_l0_attn_inputs(inputs))
    host.update(host_l0_ffn_inputs(inputs))
    host.update(host_l1_attn_inputs(inputs))
    host.update(host_l1_ffn_inputs(inputs))
    for k in _PASSTHRU:
        host[k] = np.ascontiguousarray(np.asarray(inputs[k], dtype=np.float32))
    return host


def kernel(**inputs):
    host = _host_inputs(inputs)
    x = np.asarray(inputs["L0_attn_hidden_states"], dtype=np.float32)
    xs = [np.ascontiguousarray(x[i].T) for i in range(8)]

    if FUSED:
        nc = _build_fused()
        base = dict(host)
        in_maps = [{**base, "xin": xs[i]} for i in range(8)]
        res = run_bass_kernel_spmd(nc, in_maps, core_ids=list(range(8)))
        out = np.stack([np.asarray(res.results[i]["xout"]).T for i in range(8)])
        return np.ascontiguousarray(out, dtype=np.float32)

    cur = xs
    for which in _BLOCKS:
        nc = _build_block(which)
        keys = _KEYS[which]
        base = {k: host[k] for k in keys}
        base.update({k: host[k] for k in host_constants()})
        in_maps = [{**base, "xin": cur[i]} for i in range(8)]
        res = run_bass_kernel_spmd(nc, in_maps, core_ids=list(range(8)))
        cur = [np.ascontiguousarray(np.asarray(res.results[i]["xout"])) for i in range(8)]
    out = np.stack([c.T for c in cur])
    return np.ascontiguousarray(out, dtype=np.float32)


# version 15 (final)
# speedup vs baseline: 1.0173x; 1.0173x over previous
import contextlib
import numpy as np
import concourse.bass as bass
import concourse.mybir as mybir

F32 = mybir.dt.float32
BF16 = mybir.dt.bfloat16
AF = mybir.ActivationFunctionType
ALU = mybir.AluOpType
AX = mybir.AxisListType

D = 1024
T = 2048
NCH = 8
NEG = -30000.0

ENGS = ("tensor", "vector", "scalar", "gpsimd", "sync")
SELF_EDGES = False
FIFO_ENGS = ("tensor", "vector")
DQ = {"qs": ("sync", 8), "qg": ("gpsimd", 6), "qt": ("sync", 2)}


def _uniquify(nc):
    if getattr(nc, "_uniq_wrapped", False):
        return
    nc._uniq_wrapped = True
    seen = {}

    def wrap(orig):
        def f(name, shape, dt, *a, **k):
            n = seen.get(name, 0)
            seen[name] = n + 1
            nm = name if n == 0 else f"{name}__{n}"
            return orig(nm, shape, dt, *a, **k)
        return f
    nc.sbuf_tensor = wrap(nc.sbuf_tensor)
    nc.psum_tensor = wrap(nc.psum_tensor)


class Sync:

    def __init__(self, nc, stack):
        _uniquify(nc)
        self.force_raw = True
        self.nc = nc
        self.sem = {}
        self.cnt = {}
        for e in ENGS:
            self.sem[e] = stack.enter_context(nc.semaphore(f"os_{e}"))
            self.cnt[e] = 0
        self.qsub = {}
        self.qrr = {}
        for q, (eng, r) in DQ.items():
            self.qsub[q] = []
            self.qrr[q] = 0
            for i in range(r):
                s = f"{q}{i}"
                self.sem[s] = stack.enter_context(nc.semaphore(f"od_{s}"))
                self.cnt[s] = 0
                self.qsub[q].append(s)
        self.streams = list(ENGS) + [s for q in DQ for s in self.qsub[q]]
        self.wm = {a: {b: 0 for b in self.streams} for a in ENGS}
        self.writers = {}
        self.readers = {}

    def _eng(self, e):
        return getattr(self.nc, e)

    def _need_wait(self, eng, dep_s, dep_c):
        if dep_c <= 0:
            return False
        return self.wm[eng][dep_s] < dep_c

    def _emit_wait(self, eng, dep_s, dep_c):
        if self._need_wait(eng, dep_s, dep_c):
            self._eng(eng).wait_ge(self.sem[dep_s], dep_c)
            self.wm[eng][dep_s] = dep_c

    @staticmethod
    def _names(aps):
        out = []
        for a in aps:
            if a is None:
                continue
            t = getattr(a, "tensor", a)
            nm = t.name if hasattr(t, "name") else str(t)
            out.append((nm, type(t).__name__ == "PSumTensorHandle"))
        return out

    def _skip(self, eng, s, skip_self):
        if s != eng:
            return False
        if skip_self or s == "tensor":
            return True
        if getattr(self, "force_raw", False):
            return False
        return (not SELF_EDGES) and s in FIFO_ENGS

    def _pre(self, eng, reads, writes, skip_self=False):
        for (t, is_ps) in self._names(reads):
            for w in self.writers.get(t, ()):
                s, c = w[0], w[1]
                fragile = len(w) > 2 and w[2]
                if (not fragile) and self._skip(eng, s, skip_self):
                    continue
                self._emit_wait(eng, s, c)
            if is_ps:
                for (s, c) in self.readers.get(t, ()):
                    if s != eng and s in ("vector", "scalar"):
                        self._emit_wait(eng, s, c)
        for (t, is_ps) in self._names(writes):
            for w in self.writers.get(t, ()):
                s, c = w[0], w[1]
                if s == eng and (skip_self or s == "tensor"):
                    continue
                self._emit_wait(eng, s, c)
            for (s, c) in self.readers.get(t, ()):
                if s == eng and (skip_self or s == "tensor"):
                    continue
                self._emit_wait(eng, s, c)

    def _post(self, stream, reads, writes, n, fragile=False):
        for (t, _) in self._names(writes):
            self.writers[t] = [(stream, n, fragile)]
            self.readers[t] = []
        for (t, _) in self._names(reads):
            self.readers.setdefault(t, []).append((stream, n))

    def op(self, eng, fn, reads=(), writes=(), skip_self=False, fragile=False):
        self._pre(eng, reads, writes, skip_self=skip_self)
        inst = fn(self._eng(eng))
        self.cnt[eng] += 1
        inst.then_inc(self.sem[eng], 1)
        self._post(eng, reads, writes, self.cnt[eng], fragile=fragile)
        return inst

    def dma(self, q, out, in_, transpose=False, accum=False, wkey=None, rkey=None):
        eng = DQ[q][0]
        sub = self.qsub[q][self.qrr[q] % len(self.qsub[q])]
        self.qrr[q] += 1
        self._emit_wait(eng, sub, self.cnt[sub])
        rd = [rkey] if rkey is not None else [in_]
        wr = [wkey] if wkey is not None else [out]
        self._pre(eng, rd, wr)
        e = self._eng(eng)
        if transpose:
            inst = e.dma_start_transpose(out, in_)
        elif accum:
            inst = e.dma_start(out, in_, accum_op=ALU.add)
        else:
            inst = e.dma_start(out, in_)
        self.cnt[sub] += 16
        inst.then_inc(self.sem[sub], 16)
        self._post(sub, rd, wr, self.cnt[sub])
        return inst

    def barrier(self):
        nc = self.nc
        subs = [s for q in DQ for s in self.qsub[q]]
        for s in subs:
            self._emit_wait("sync", s, self.cnt[s])
        nc.all_engine_barrier()
        for a in ENGS:
            for b in ENGS:
                self.wm[a][b] = self.cnt[b]
            for s in subs:
                self.wm[a][s] = self.cnt[s]

    def mm(self, out, lhsT, rhs, start=True, stop=True, tile_position=None):
        return self.op("tensor",
                       lambda e: e.matmul(out, lhsT, rhs, start=start, stop=stop,
                                          tile_position=tile_position),
                       reads=[lhsT, rhs], writes=[out], skip_self=True)

    def act(self, out, in_, func, bias=0.0, scale=1.0, accum_out=None):
        rd = [in_]
        if isinstance(bias, bass.AP):
            rd.append(bias)
        if isinstance(scale, bass.AP):
            rd.append(scale)
        wr = [out] + ([accum_out] if accum_out is not None else [])
        return self.op("scalar",
                       lambda e: e.activation(out, in_, func, bias=bias, scale=scale,
                                              accum_out=accum_out),
                       reads=rd, writes=wr)

    def copy(self, eng, out, in_):
        return self.op(eng, lambda e: e.tensor_copy(out, in_), reads=[in_], writes=[out])

    def scopy(self, out, in_):
        return self.op("scalar", lambda e: e.copy(out, in_), reads=[in_], writes=[out])

    def tt(self, out, in0, in1, alu, eng="vector"):
        return self.op(eng, lambda e: e.tensor_tensor(out, in0, in1, alu),
                       reads=[in0, in1], writes=[out])

    def ts(self, out, in0, s1, s2, op0, op1=None, eng="vector"):
        rd = [in0] + [x for x in (s1, s2) if isinstance(x, bass.AP)]
        if op1 is None:
            return self.op(eng, lambda e: e.tensor_single_scalar(out, in0, s1, op0),
                           reads=rd, writes=[out])
        return self.op(eng, lambda e: e.tensor_scalar(out, in0, s1, s2, op0, op1),
                       reads=rd, writes=[out])

    def stt(self, out, in0, scalar, in1, op0, op1, eng="vector"):
        rd = [in0, in1] + ([scalar] if isinstance(scalar, bass.AP) else [])
        return self.op(eng, lambda e: e.scalar_tensor_tensor(out, in0, scalar, in1, op0, op1),
                       reads=rd, writes=[out])

    def red(self, out, in_, alu, eng="vector", negate=False):
        return self.op(eng, lambda e: e.tensor_reduce(out, in_, AX.X, alu, negate=negate),
                       reads=[in_], writes=[out], fragile=True)

    def memset(self, out, val, eng="vector"):
        return self.op(eng, lambda e: e.memset(out, val), reads=[], writes=[out])

    def recip(self, out, in_):
        return self.op("vector", lambda e: e.reciprocal(out, in_), reads=[in_], writes=[out],
                       fragile=True)

    def rowpow(self, buf, scale):
        self.act(buf, buf, AF.Ln)
        self.act(buf, buf, AF.Exp, scale=scale)


class Consts:

    def __init__(self, nc, S, stack, dts):
        sb = lambda n, sh, dt: stack.enter_context(nc.sbuf_tensor(n, sh, dt))
        self.ones_col_bf = sb("k_ones_col_bf", [128, 1], BF16)
        self.ones_row_bf = sb("k_ones_row_bf", [1, 512], BF16)
        self.ones_row_f = sb("k_ones_row_f", [1, 128], F32)
        self.t_lo = sb("k_t_lo", [128, 128], F32)
        self.t_up = sb("k_t_up", [128, 128], F32)
        self.ident_bf = sb("k_ident_bf", [128, 128], BF16)
        self.ident_f = sb("k_ident_f", [128, 128], F32)
        self.ones_col_f = sb("k_ones_col_f", [128, 1], F32)
        S.memset(self.ones_col_bf[:], 1.0)
        S.memset(self.ones_col_f[:], 1.0)
        S.memset(self.ones_row_bf[:], 1.0)
        S.memset(self.ones_row_f[:], 1.0)
        S.dma("qs", self.t_lo[:], dts["c_t_lo"].ap())
        S.dma("qs", self.t_up[:], dts["c_t_up"].ap())
        S.dma("qs", self.ident_bf[:], dts["c_ident"].ap())
        S.dma("qs", self.ident_f[:], dts["c_ident_f"].ap())


def alloc_chunks(nc, stack, name, nch, shape, dt):
    return [stack.enter_context(nc.sbuf_tensor(f"{name}{i}", shape, dt))
            for i in range(nch)]


def emit_ln(nc, S, C, stack, xin, out_chunks, out_dt, tile=512, out_dram=None):
    with contextlib.ExitStack() as st:
        xs = alloc_chunks(nc, st, "ln_x", NCH, [128, tile], F32)
        xs2 = alloc_chunks(nc, st, "ln_x2", NCH, [128, tile], F32)
        xb = alloc_chunks(nc, st, "ln_xb", 2, [128, tile], BF16)
        sq = alloc_chunks(nc, st, "ln_sq", 2, [128, tile], BF16)
        tmp = alloc_chunks(nc, st, "ln_tmp", 2, [128, tile], F32)
        tmp2 = alloc_chunks(nc, st, "ln_tmp2", 2, [128, tile], F32) if out_dram is not None else None
        rowa2 = [st.enter_context(nc.sbuf_tensor(f"ln_rowa{i}", [1, tile], F32)) for i in range(2)]
        rowb2 = [st.enter_context(nc.sbuf_tensor(f"ln_rowb{i}", [1, tile], F32)) for i in range(2)]
        rowc2 = [st.enter_context(nc.sbuf_tensor(f"ln_rowc{i}", [1, tile], F32)) for i in range(2)]
        ps_s2 = [st.enter_context(nc.psum_tensor(f"ln_ps_s{i}", [1, tile], F32)) for i in range(2)]
        ps_q2 = [st.enter_context(nc.psum_tensor(f"ln_ps_q{i}", [1, tile], F32)) for i in range(2)]
        ps_a2 = [st.enter_context(nc.psum_tensor(f"ln_ps_a{i}", [128, tile], F32)) for i in range(2)]
        ps_b2 = [st.enter_context(nc.psum_tensor(f"ln_ps_b{i}", [128, tile], F32)) for i in range(2)]
        ntile = T // tile
        S.force_raw = False
        for t in range(ntile):
            rowa, rowb, rowc = rowa2[t % 2], rowb2[t % 2], rowc2[t % 2]
            ps_s, ps_q, ps_a, ps_b = ps_s2[t % 2], ps_q2[t % 2], ps_a2[t % 2], ps_b2[t % 2]
            xcur = xs if t % 2 == 0 else xs2
            for c in range(NCH):
                S.dma("qs", xcur[c][:], xin[c * 128:(c + 1) * 128, t * tile:(t + 1) * tile])
            for c in range(NCH):
                b = xb[c % 2]
                s = sq[c % 2]
                S.copy("vector", b[:], xcur[c][:])
                S.tt(s[:], b[:], b[:], ALU.mult, eng="gpsimd")
                S.mm(ps_s[:], C.ones_col_bf[:], b[:], start=(c == 0), stop=(c == NCH - 1))
                S.mm(ps_q[:], C.ones_col_bf[:], s[:], start=(c == 0), stop=(c == NCH - 1))
            S.ts(rowa[:], ps_s[:], 1.0 / D, None, ALU.mult)
            S.ts(rowb[:], ps_q[:], 1.0 / D, None, ALU.mult)
            S.tt(rowc[:], rowa[:], rowa[:], ALU.mult)
            S.tt(rowb[:], rowb[:], rowc[:], ALU.subtract)
            S.ts(rowb[:], rowb[:], 1e-5, None, ALU.add)
            S.rowpow(rowb[:], -0.5)
            S.stt(rowc[:], rowa[:], -1.0, rowb[:], ALU.mult, ALU.mult)
            S.mm(ps_a[:], C.ones_row_f[:], rowb[:])
            S.mm(ps_b[:], C.ones_row_f[:], rowc[:])
            for c in range(NCH):
                tm = tmp[c % 2]
                S.tt(tm[:], xcur[c][:], ps_a[:], ALU.mult)
                if out_dram is None:
                    S.tt(out_chunks[c][:, t * tile:(t + 1) * tile], tm[:], ps_b[:], ALU.add)
                else:
                    to = tmp2[c % 2]
                    S.tt(to[:], tm[:], ps_b[:], ALU.add)
                    S.dma("qs", out_dram[c * 128:(c + 1) * 128, t * tile:(t + 1) * tile], to[:])
        S.force_raw = True
        S.barrier()



def host_constants():
    import ml_dtypes
    i = np.arange(128)
    t_lo = np.where(i[:, None] > i[None, :], np.float32(NEG), np.float32(0.0)).astype(np.float32)
    t_up = t_lo.T.copy()
    ident = np.eye(128).astype(ml_dtypes.bfloat16)
    return {"c_t_lo": t_lo, "c_t_up": t_up, "c_ident": ident,
            "c_ident_f": np.eye(128, dtype=np.float32)}



def emit_l0_attn(nc, S, C, stack, xin, xout, W):
    NH = 16
    GH = 2
    NGR = NH // GH

    with contextlib.ExitStack() as st:
        xn = alloc_chunks(nc, st, "a0_xn", NCH, [128, T], BF16)
        emit_ln(nc, S, C, st, xin, xn, BF16)

        bbf = {}
        for nm, key in (("q", "L0_attn_bq"), ("k", "L0_attn_bk"), ("v", "L0_attn_bv")):
            bc = st.enter_context(nc.sbuf_tensor(f"a0_b{nm}c", [128, 8], F32))
            bbf[nm] = st.enter_context(nc.sbuf_tensor(f"a0_b{nm}b", [128, 8], BF16))
            for c in range(8):
                S.dma("qs", bc[:, c], W[key].ap()[c * 128:(c + 1) * 128])
            S.scopy(bbf[nm][:], bc[:])
        cfl = st.enter_context(nc.sbuf_tensor("a0_cfl", [1, 32], F32))
        vbo = st.enter_context(nc.sbuf_tensor("a0_vbo", [1, 2080], F32))
        S.dma("qs", cfl[:], W["cfl"].ap())
        S.dma("qs", vbo[:], W["vb_ones"].ap())

        o_fm = alloc_chunks(nc, st, "a0_o", NCH, [128, T], BF16)

        S.force_raw = False
        with contextlib.ExitStack() as gs:
            a_bf = alloc_chunks(nc, gs, "a0_a", NCH, [128, 130], BF16)
            weq = alloc_chunks(nc, gs, "a0_weq", NCH, [128, 256], BF16)
            wek = alloc_chunks(nc, gs, "a0_wek", NCH, [128, 256], BF16)
            wev = alloc_chunks(nc, gs, "a0_wev", NCH, [128, 260], BF16)
            beqs = gs.enter_context(nc.sbuf_tensor("a0_beqs", [128, 2], F32))
            bek = gs.enter_context(nc.sbuf_tensor("a0_bek", [128, 2], F32))
            bvr = gs.enter_context(nc.sbuf_tensor("a0_bvr", [1, 260], BF16))
            q_aug = [[gs.enter_context(nc.sbuf_tensor(f"a0_q{h}_{p}", [65, T], BF16))
                      for p in range(2)] for h in range(GH)]
            k_aug = [gs.enter_context(nc.sbuf_tensor(f"a0_k{h}", [65, 2 * T], BF16))
                     for h in range(GH)]
            v_sb = alloc_chunks(nc, gs, "a0_v", 16, [128, 260], BF16)
            sqq = [gs.enter_context(nc.sbuf_tensor(f"a0_sqq{i}", [128, 512], BF16)) for i in range(2)]
            nrow = gs.enter_context(nc.sbuf_tensor("a0_nr", [1, 4096], F32))
            maxk = [gs.enter_context(nc.sbuf_tensor(f"a0_mk{h}", [1, 1], F32)) for h in range(GH)]
            pb = [gs.enter_context(nc.sbuf_tensor(f"a0_pb{i}", [128, 512], BF16)) for i in range(4)]
            bcs = [gs.enter_context(nc.sbuf_tensor(f"a0_bcs{i}", [64, 512], F32)) for i in range(2)]
            tmp0 = gs.enter_context(nc.sbuf_tensor("a0_tmp0", [64, 512], F32))
            tmp2 = gs.enter_context(nc.sbuf_tensor("a0_tmp2", [64, 512], F32))
            oraw = [gs.enter_context(nc.sbuf_tensor(f"a0_oraw{i}", [64, T], BF16)) for i in range(2)]

            ps_f = [gs.enter_context(nc.psum_tensor(f"a0_psf{i}", [128, 512], F32)) for i in range(2)]
            ps_misc = gs.enter_context(nc.psum_tensor("a0_psmisc", [128, 512], F32))
            ps_b = ps_misc
            ps_n = ps_misc
            ps_s = [gs.enter_context(nc.psum_tensor(f"a0_pss{i}", [128, 512], F32)) for i in range(2)]
            ps_avs = [gs.enter_context(nc.psum_tensor(f"a0_psav{i}", [65, 512], F32)) for i in range(2)]
            ps_bc = gs.enter_context(nc.psum_tensor("a0_psbc", [64, 512], F32))

            weq_d = nc.dram_tensor("a0_weq_scr", [1024, 2048], BF16).ap()
            wek_d = nc.dram_tensor("a0_wek_scr", [1024, 2048], BF16).ap()
            wev_d = nc.dram_tensor("a0_wev_scr", [1024, 2080], BF16).ap()
            with contextlib.ExitStack() as fs_:
                w_bf = alloc_chunks(nc, fs_, "a0_w", NCH, [128, 1024], BF16)
                a_big = alloc_chunks(nc, fs_, "a0_abig", NCH, [128, 512], BF16)
                fout = [fs_.enter_context(nc.sbuf_tensor(f"a0_fo{i}", [128, 512], BF16))
                        for i in range(2)]
                for (wkey, akey, wd, wtot) in (("L0_attn_Wq", "A_q", weq_d, 2048),
                                               ("L0_attn_Wk", "A_k", wek_d, 2048),
                                               ("L0_attn_Wv", "A_v", wev_d, 2080)):
                    for r in range(NCH):
                        S.dma("qg", w_bf[r][:], W[wkey].ap()[r * 128:(r + 1) * 128, :])
                    for nt in range((wtot + 511) // 512):
                        nw = min(512, wtot - nt * 512)
                        for r in range(NCH):
                            S.dma("qg", a_big[r][:, 0:nw],
                                  W[akey].ap()[r * 128:(r + 1) * 128, nt * 512:nt * 512 + nw])
                        for m in range(NCH):
                            for r in range(NCH):
                                S.mm(ps_f[m % 2][:, 0:nw], w_bf[r][:, m * 128:(m + 1) * 128],
                                     a_big[r][:, 0:nw], start=(r == 0), stop=(r == NCH - 1))
                            fo = fout[m % 2]
                            S.copy("vector", fo[:, 0:nw], ps_f[m % 2][:, 0:nw])
                            S.dma("qs", wd[m * 128:(m + 1) * 128, nt * 512:nt * 512 + nw],
                                  fo[:, 0:nw])
                S.barrier()

            for g in range(NGR):
                for (akey, we, wid, bias_nm) in (("A_q", weq, 128, "q"),
                                                 ("A_k", wek, 128, "k"),
                                                 ("A_v", wev, 130, "v")):
                    stride = 1024 if wid == 128 else 1040
                    wd = {"q": weq_d, "k": wek_d, "v": wev_d}[bias_nm]
                    for p in range(2):
                        base = p * stride + g * wid
                        for r in range(NCH):
                            S.dma("qs", we[r][:, p * wid:(p + 1) * wid],
                                  wd[r * 128:(r + 1) * 128, base:base + wid])
                        for r in range(NCH):
                            S.dma("qg", a_bf[r][:, 0:wid],
                                  W[akey].ap()[r * 128:(r + 1) * 128, base:base + wid])
                        if bias_nm in ("q", "k"):
                            bout = beqs if bias_nm == "q" else bek
                            for r in range(NCH):
                                S.mm(ps_b[:, 0:1], a_bf[r][:, 0:128],
                                     bbf[bias_nm][:, r:r + 1],
                                     start=(r == 0), stop=(r == NCH - 1))
                            sc = 0.125 if bias_nm == "q" else 1.0
                            S.op("scalar", lambda e, o=bout[:, p:p + 1],
                                 i=ps_b[:, 0:1], s=sc: e.mul(o, i, s),
                                 reads=[ps_b], writes=[bout])
                        else:
                            for r in range(NCH):
                                S.mm(ps_n[0:1, 0:130], bbf["v"][:, r:r + 1], a_bf[r][:, 0:130],
                                     start=(r == 0), stop=(r == NCH - 1))
                            S.tt(bvr[:, p * 130:(p + 1) * 130], ps_n[0:1, 0:130],
                                 vbo[:, p * 1040 + g * 130: p * 1040 + (g + 1) * 130], ALU.add)

                for p in range(2):
                    for tc in range(16):
                        pv = ps_f[tc % 2]
                        for m in range(NCH):
                            S.mm(pv[:, 0:130], xn[m][:, tc * 128:(tc + 1) * 128],
                                 wev[m][:, p * 130:(p + 1) * 130],
                                 start=(m == 0), stop=False)
                        S.mm(pv[:, 0:130], C.ones_row_bf[0:1, 0:128],
                             bvr[0:1, p * 130:(p + 1) * 130], start=False, stop=True)
                        S.copy("vector", v_sb[tc][:, p * 130:(p + 1) * 130], pv[:, 0:130])

                for hh in range(GH):
                    S.memset(k_aug[hh][64:65, :], 1.0)
                for p in range(2):
                    for t4 in range(4):
                        pk_ = ps_f[t4 % 2]
                        for m in range(NCH):
                            S.mm(pk_[:, 0:512], wek[m][:, p * 128:(p + 1) * 128],
                                 xn[m][:, t4 * 512:(t4 + 1) * 512],
                                 start=(m == 0), stop=(m == NCH - 1))
                        for hh in range(GH):
                            S.ts(k_aug[hh][0:64, p * T + t4 * 512: p * T + (t4 + 1) * 512],
                                 pk_[hh * 64:(hh + 1) * 64, 0:512],
                                 bek[hh * 64:(hh + 1) * 64, p:p + 1], None, ALU.add)
                for hh in range(GH):
                    for p in range(2):
                        for t4 in range(4):
                            kslc = k_aug[hh][0:64, p * T + t4 * 512: p * T + (t4 + 1) * 512]
                            S.tt(sqq[0][0:64, 0:512], kslc, kslc, ALU.mult, eng="gpsimd")
                            S.mm(ps_n[0:1, 0:512], C.ones_col_bf[0:64, :], sqq[0][0:64, 0:512])
                            S.copy("vector", nrow[:, p * T + t4 * 512: p * T + (t4 + 1) * 512],
                                   ps_n[0:1, 0:512])
                    S.rowpow(nrow[:], 0.5)
                    S.red(maxk[hh][:], nrow[:], ALU.max)

                for p in range(2):
                    for t4 in range(4):
                        pq_ = ps_f[t4 % 2]
                        for m in range(NCH):
                            S.mm(pq_[:, 0:512], weq[m][:, p * 128:(p + 1) * 128],
                                 xn[m][:, t4 * 512:(t4 + 1) * 512],
                                 start=(m == 0), stop=(m == NCH - 1))
                        for hh in range(GH):
                            S.ts(q_aug[hh][p][0:64, t4 * 512:(t4 + 1) * 512],
                                 pq_[hh * 64:(hh + 1) * 64, 0:512],
                                 0.125, beqs[hh * 64:(hh + 1) * 64, p:p + 1],
                                 ALU.mult, ALU.add)
                    for hh in range(GH):
                        for t4 in range(4):
                            qslc = q_aug[hh][p][0:64, t4 * 512:(t4 + 1) * 512]
                            S.tt(sqq[t4 % 2][0:64, 0:512], qslc, qslc, ALU.mult, eng="gpsimd")
                            S.mm(ps_n[0:1, 0:512], C.ones_col_bf[0:64, :], sqq[t4 % 2][0:64, 0:512])
                            S.copy("vector", nrow[:, t4 * 512:(t4 + 1) * 512], ps_n[0:1, 0:512])
                        S.rowpow(nrow[0:1, 0:T], 0.5)
                        S.ts(q_aug[hh][p][64:65, :], nrow[0:1, 0:T], maxk[hh][0:1, 0:1], -1.0,
                             ALU.mult, ALU.mult)

                sbanks = [ps_s[0], ps_s[1], ps_f[0], ps_f[1]]
                for hh in range(GH):
                    h = g * GH + hh
                    for p in range(2):
                        orawp = oraw[p]
                        for qt in range(4):
                            ps_av = ps_avs[qt % 2]
                            qsl = q_aug[hh][p][:, qt * 512:(qt + 1) * 512]
                            chunks = [(pk, c) for pk in range(2) for c in range(4 * qt + 4)]
                            nck = len(chunks)

                            def emit_score(i):
                                pk, c = chunks[i]
                                cr = c - 4 * qt
                                n0 = cr * 128 if cr > 0 else 0
                                pss = sbanks[i % 4]
                                S.mm(pss[:, n0:512],
                                     k_aug[hh][:, pk * T + c * 128: pk * T + (c + 1) * 128],
                                     qsl[:, n0:512] if n0 else qsl)
                                if cr >= 0:
                                    S.tt(pss[:, cr * 128:(cr + 1) * 128],
                                         pss[:, cr * 128:(cr + 1) * 128], C.t_lo[:], ALU.add)
                                prb = pb[i % 4]
                                if n0:
                                    S.memset(prb[:, 0:n0], 0.0)
                                S.act(prb[:, n0:512], pss[:, n0:512], AF.Exp)

                            def emit_av(i):
                                pk, c = chunks[i]
                                S.mm(ps_av[:, 0:512],
                                     v_sb[c][:, pk * 130 + hh * 65: pk * 130 + (hh + 1) * 65],
                                     pb[i % 4][:], start=(i == 0), stop=(i == nck - 1))

                            for i in range(nck):
                                emit_score(i)
                                if i >= 3:
                                    emit_av(i - 3)
                            emit_av(nck - 3)
                            emit_av(nck - 2)
                            emit_av(nck - 1)
                            S.copy("vector", orawp[:, qt * 512:(qt + 1) * 512],
                                   ps_av[0:64, 0:512])
                            S.copy("vector", nrow[0:1, p * T + qt * 512: p * T + (qt + 1) * 512],
                                   ps_av[64:65, 0:512])
                        S.rowpow(nrow[0:1, p * T:(p + 1) * T], -1.0)
                        S.ts(nrow[0:1, p * T:(p + 1) * T], nrow[0:1, p * T:(p + 1) * T],
                             cfl[0:1, h * 2 + p: h * 2 + p + 1], None, ALU.mult)
                    for qt in range(4):
                        tsl = slice(qt * 512, (qt + 1) * 512)
                        for p in range(2):
                            S.mm(ps_bc[:, 0:512], C.ones_row_f[0:1, 0:64],
                                 nrow[0:1, p * T + qt * 512: p * T + (qt + 1) * 512])
                            S.copy("vector", bcs[p][:], ps_bc[:, 0:512])
                        S.tt(tmp0[:], oraw[0][:, tsl], bcs[0][:], ALU.mult)
                        S.tt(tmp2[:], oraw[1][:, tsl], bcs[1][:], ALU.mult)
                        osl = o_fm[h // 2][(h % 2) * 64:(h % 2) * 64 + 64, tsl]
                        S.tt(osl, tmp0[:], tmp2[:], ALU.add)
            S.barrier()

        with contextlib.ExitStack() as os_:
            wo = alloc_chunks(nc, os_, "a0_wo", NCH, [128, 1024], BF16)
            bo_row = os_.enter_context(nc.sbuf_tensor("a0_bor", [1, 1024], F32))
            bo_bf = os_.enter_context(nc.sbuf_tensor("a0_bob", [1, 1024], BF16))
            S.dma("qs", bo_row[0:1, :], W["L0_attn_bo"].ap()[None, :])
            S.scopy(bo_bf[:], bo_row[:])
            for r in range(NCH):
                S.dma("qg", wo[r][:], W["WoT"].ap()[r * 128:(r + 1) * 128, :])
            xt = [os_.enter_context(nc.sbuf_tensor(f"a0_xt{i}", [128, 512], F32)) for i in range(2)]
            xo = [os_.enter_context(nc.sbuf_tensor(f"a0_xo{i}", [128, 512], F32)) for i in range(2)]
            ps_o = [os_.enter_context(nc.psum_tensor(f"a0_pso{i}", [128, 512], F32)) for i in range(2)]
            for dm in range(NCH):
                for t4 in range(4):
                    pso = ps_o[(dm * 4 + t4) % 2]
                    for ic in range(NCH):
                        S.mm(pso[:, 0:512], wo[ic][:, dm * 128:(dm + 1) * 128],
                             o_fm[ic][:, t4 * 512:(t4 + 1) * 512], start=(ic == 0), stop=False)
                    S.mm(pso[:, 0:512], bo_bf[0:1, dm * 128:(dm + 1) * 128],
                         C.ones_row_bf[0:1, 0:512], start=False, stop=True)
                    i2 = (dm * 4 + t4) % 2
                    S.dma("qs", xt[i2][:], xin[dm * 128:(dm + 1) * 128, t4 * 512:(t4 + 1) * 512])
                    S.tt(xo[i2][:], pso[:, 0:512], xt[i2][:], ALU.add)
                    S.dma("qs", xout[dm * 128:(dm + 1) * 128, t4 * 512:(t4 + 1) * 512], xo[i2][:])
            S.force_raw = True
            S.barrier()


def host_l0_attn_inputs(ins):
    H, P, Dh = 16, 2, 64
    aq, ak, av = ins["L0_attn_alpha_q"], ins["L0_attn_alpha_k"], ins["L0_attn_alpha_v"]
    A_q = np.zeros((1024, 2048), np.float32)
    A_k = np.zeros((1024, 2048), np.float32)
    A_v = np.zeros((1024, 2080), np.float32)
    dd = np.arange(Dh)
    for m in range(H):
        for h in range(H):
            for p in range(P):
                A_q[m * Dh + dd, p * 1024 + h * Dh + dd] = aq[m, h, p]
                A_k[m * Dh + dd, p * 1024 + h * Dh + dd] = ak[m, h, p]
                A_v[m * Dh + dd, p * 1040 + h * 65 + dd] = av[m, h, p]
    vb = np.zeros((1, 2080), np.float32)
    for p in range(P):
        for h in range(H):
            vb[0, p * 1040 + h * 65 + 64] = 1.0
    return {
        "A_q": A_q, "A_k": A_k, "A_v": A_v,
        "vb_ones": vb,
        "cfl": np.ascontiguousarray(ins["L0_attn_collapse"].reshape(1, 32)),
        "WoT": np.ascontiguousarray(np.asarray(ins["L0_attn_Wo"]).T),
    }


def decl_l0_attn_weights(nc):
    dts = {}
    for nm, sh in (("L0_attn_Wq", (1024, 1024)), ("L0_attn_Wk", (1024, 1024)),
                   ("L0_attn_Wv", (1024, 1024)), ("WoT", (1024, 1024)),
                   ("L0_attn_bq", (1024,)), ("L0_attn_bk", (1024,)),
                   ("L0_attn_bv", (1024,)), ("L0_attn_bo", (1024,)),
                   ("A_q", (1024, 2048)), ("A_k", (1024, 2048)), ("A_v", (1024, 2080)),
                   ("vb_ones", (1, 2080)), ("cfl", (1, 32))):
        dts[nm] = nc.dram_tensor(nm, list(sh), F32, kind="ExternalInput")
    return dts



def emit_l0_ffn(nc, S, C, stack, xin, xout, W, nexp=10, flags=()):
    E = 8
    with contextlib.ExitStack() as st:
        xn = alloc_chunks(nc, st, "f0_xn", NCH, [128, T], BF16)
        emit_ln(nc, S, C, st, xin, xn, BF16)

        for c in range(NCH):
            S.dma("qs", xout[c * 128:(c + 1) * 128, :], xin[c * 128:(c + 1) * 128, :],
                  wkey=f"xoacc{c}")

        skipr = "skipr" in flags
        gate_fm = st.enter_context(nc.sbuf_tensor("f0_gfm", [8, T], F32))
        gate_bf = st.enter_context(nc.sbuf_tensor("f0_gbf", [8, T], BF16))
        with contextlib.ExitStack() as rs:
            gwT = alloc_chunks(nc, rs, "f0_gw", NCH, [128, 8], BF16)
            for r in range(NCH):
                S.dma("qg", gwT[r][:], W["gwT"].ap()[r * 128:(r + 1) * 128, :])
            gbrow = rs.enter_context(nc.sbuf_tensor("f0_gb", [1, 8], F32))
            gbb = rs.enter_context(nc.sbuf_tensor("f0_gbb", [1, 8], BF16))
            S.dma("qs", gbrow[:], W["L0_ffn_gate_b"].ap()[None, :])
            S.scopy(gbb[:], gbrow[:])
            lg = [rs.enter_context(nc.sbuf_tensor(f"f0_lg{i}", [128, 8], F32)) for i in range(2)]
            e8 = rs.enter_context(nc.sbuf_tensor("f0_e8", [128, 8], F32))
            keep = rs.enter_context(nc.sbuf_tensor("f0_keep", [128, 8], F32))
            l2 = rs.enter_context(nc.sbuf_tensor("f0_l2", [128, 8], F32))
            m1 = rs.enter_context(nc.sbuf_tensor("f0_m1", [128, 1], F32))
            m2 = rs.enter_context(nc.sbuf_tensor("f0_m2", [128, 1], F32))
            gt = rs.enter_context(nc.sbuf_tensor("f0_gt", [128, 8], F32))
            ps_l = [rs.enter_context(nc.psum_tensor(f"f0_psl{i}", [128, 8], F32)) for i in range(2)]
            ps_t = [rs.enter_context(nc.psum_tensor(f"f0_pst{i}", [8, 128], F32)) for i in range(2)]
            for tc in range(0 if skipr else 16):
                i = tc % 2
                pl = ps_l[i]
                for m in range(NCH):
                    S.mm(pl[:], xn[m][:, tc * 128:(tc + 1) * 128], gwT[m][:],
                         start=(m == 0), stop=False)
                S.mm(pl[:], C.ones_row_bf[0:1, 0:128], gbb[:], start=False, stop=True)
                L = lg[i]
                S.copy("vector", L[:], pl[:])
                if "r1" in flags:
                    continue
                S.red(m1[:], L[:], ALU.max)
                S.ts(keep[:], L[:], m1[:], None, ALU.is_ge)
                S.stt(l2[:], keep[:], NEG, L[:], ALU.mult, ALU.add)
                S.red(m2[:], l2[:], ALU.max)
                S.ts(keep[:], L[:], m2[:], None, ALU.is_ge)
                S.ts(m1[:], m1[:], -1.0, None, ALU.mult)
                if "r2" in flags:
                    continue
                S.act(e8[:], L[:], AF.Exp, bias=m1[:])
                S.red(m2[:], e8[:], ALU.add)
                S.recip(m2[:], m2[:])
                S.tt(gt[:], e8[:], keep[:], ALU.mult)
                S.ts(gt[:], gt[:], m2[:], None, ALU.mult)
                if "r3" in flags:
                    continue
                S.op("tensor", lambda e, o=ps_t[i][:], a=gt[:], idn=C.ident_f[:]:
                     e.transpose(o, a, idn), reads=[gt, C.ident_f], writes=[ps_t[i]])
                if "r4" in flags:
                    continue
                S.copy("vector", gate_fm[:, tc * 128:(tc + 1) * 128], ps_t[i][:])
                if "r5" in flags:
                    continue
                S.copy("gpsimd", gate_bf[:, tc * 128:(tc + 1) * 128],
                       gate_fm[:, tc * 128:(tc + 1) * 128])
            S.barrier()

        we1s = [alloc_chunks(nc, st, f"f0_we1{i}", NCH, [128, 1024], BF16) for i in range(2)]
        we3s = [alloc_chunks(nc, st, f"f0_we3{i}", NCH, [128, 1024], BF16) for i in range(2)]
        we2s = [alloc_chunks(nc, st, f"f0_we2{i}", NCH, [128, 1024], BF16) for i in range(2)]
        b1cs = [st.enter_context(nc.sbuf_tensor(f"f0_b1c{i}", [128, 8], F32)) for i in range(2)]
        b3cs = [st.enter_context(nc.sbuf_tensor(f"f0_b3c{i}", [128, 8], F32)) for i in range(2)]
        yy = [st.enter_context(nc.sbuf_tensor(f"f0_yy{i}", [128, 512], F32)) for i in range(4)]
        hh = alloc_chunks(nc, st, "f0_hh", NCH, [128, 512], BF16)
        ge_sb = st.enter_context(nc.sbuf_tensor("f0_ge", [128, T], F32))
        grow = st.enter_context(nc.sbuf_tensor("f0_grow", [1, T], F32))
        sa = [st.enter_context(nc.sbuf_tensor(f"f0_sa{i}", [128, 512], BF16)) for i in range(4)]
        bb = [st.enter_context(nc.sbuf_tensor(f"f0_bb{i}", [128, 512], BF16)) for i in range(4)]
        b2a = st.enter_context(nc.sbuf_tensor("f0_b2a", [8, 1024], BF16))
        bs2b = st.enter_context(nc.sbuf_tensor("f0_bs2", [1, 1024], BF16))
        ps_as = [st.enter_context(nc.psum_tensor(f"f0_psa{i}", [128, 512], F32)) for i in range(2)]
        ps_bs = [st.enter_context(nc.psum_tensor(f"f0_psb{i}", [128, 512], F32)) for i in range(2)]
        ps_y = [st.enter_context(nc.psum_tensor(f"f0_psy{i}", [128, 512], F32)) for i in range(2)]
        ps_g = st.enter_context(nc.psum_tensor("f0_psg", [128, 512], F32))

        if "skipb" not in flags:
            S.dma("qg", b2a[:], W["L0_ffn_b2"].ap())
            S.dma("qg", bs2b[:], W["bs2"].ap()[None, :])

        S.force_raw = False
        for e in range(nexp):
            shared = e >= E
            sh = e - E
            we1, we3, we2 = we1s[e % 2], we3s[e % 2], we2s[e % 2]
            b1c, b3c = b1cs[e % 2], b3cs[e % 2]
            for r in range(NCH):
                if not shared:
                    S.dma("qg", we1[r][:], W["w1T"].ap()[e, r * 128:(r + 1) * 128, :])
                    S.dma("qg", we3[r][:], W["w3T"].ap()[e, r * 128:(r + 1) * 128, :])
                    S.dma("qg", we2[r][:], W["w2T"].ap()[e, r * 128:(r + 1) * 128, :])
                else:
                    S.dma("qg", we1[r][:], W["ws1T"].ap()[r * 128:(r + 1) * 128,
                                                          sh * 1024:(sh + 1) * 1024])
                    S.dma("qg", we3[r][:], W["ws3T"].ap()[r * 128:(r + 1) * 128,
                                                          sh * 1024:(sh + 1) * 1024])
                    S.dma("qg", we2[r][:], W["ws2T"].ap()[sh * 1024 + r * 128:
                                                          sh * 1024 + (r + 1) * 128, :])
            for c in range(NCH):
                if not shared:
                    S.dma("qs", b1c[:, c], W["L0_ffn_b1"].ap()[e, c * 128:(c + 1) * 128])
                    S.dma("qs", b3c[:, c], W["L0_ffn_b3"].ap()[e, c * 128:(c + 1) * 128])
                else:
                    S.dma("qs", b1c[:, c], W["bs1"].ap()[sh * 1024 + c * 128: sh * 1024 + (c + 1) * 128])
                    S.dma("qs", b3c[:, c], W["bs3"].ap()[sh * 1024 + c * 128: sh * 1024 + (c + 1) * 128])
            if not shared:
                S.dma("qs", grow[0:1, :], gate_fm[e:e + 1, :])
                for t4 in range(4):
                    S.mm(ps_g[:, 0:512], C.ones_row_f[0:1, 0:128],
                         grow[0:1, t4 * 512:(t4 + 1) * 512])
                    S.copy("vector", ge_sb[:, t4 * 512:(t4 + 1) * 512], ps_g[:, 0:512])
            for t4 in range(4):
                tsl = slice(t4 * 512, (t4 + 1) * 512)
                for ic in range(NCH):
                    ps_a, ps_bp = ps_as[ic % 2], ps_bs[ic % 2]
                    for m in range(NCH):
                        S.mm(ps_a[:, 0:512], we1[m][:, ic * 128:(ic + 1) * 128],
                             xn[m][:, tsl], start=(m == 0), stop=(m == NCH - 1))
                    for m in range(NCH):
                        S.mm(ps_bp[:, 0:512], we3[m][:, ic * 128:(ic + 1) * 128],
                             xn[m][:, tsl], start=(m == 0), stop=(m == NCH - 1))
                    S.act(sa[ic % 4][:], ps_a[:, 0:512], AF.Silu, bias=b1c[:, ic:ic + 1])
                    if not shared:
                        S.stt(bb[ic % 4][:], ps_bp[:, 0:512], b3c[:, ic:ic + 1],
                              ge_sb[:, tsl], ALU.add, ALU.mult)
                    else:
                        S.ts(bb[ic % 4][:], ps_bp[:, 0:512], b3c[:, ic:ic + 1], None, ALU.add)
                    S.tt(hh[ic][:], sa[ic % 4][:], bb[ic % 4][:], ALU.mult)
                for dm in range(NCH):
                    py = ps_y[dm % 2]
                    extra = (shared and sh == 0)
                    for ic in range(NCH):
                        S.mm(py[:, 0:512], we2[ic][:, dm * 128:(dm + 1) * 128],
                             hh[ic][:], start=(ic == 0), stop=(ic == NCH - 1 and not extra))
                    if extra:
                        S.mm(py[:, 0:512], b2a[:, dm * 128:(dm + 1) * 128],
                             gate_bf[:, tsl], start=False, stop=False)
                        S.mm(py[:, 0:512], bs2b[0:1, dm * 128:(dm + 1) * 128],
                             C.ones_row_bf[0:1, 0:512], start=False, stop=True)
                    yb = yy[(dm * 4 + t4) % 4]
                    S.copy("vector", yb[:], py[:, 0:512])
                    S.dma("qg", xout[dm * 128:(dm + 1) * 128, tsl], yb[:],
                          accum=True, wkey=f"xoacc{dm}")
        S.force_raw = True
        S.barrier()


def host_l0_ffn_inputs(ins):
    return {
        "gwT": np.ascontiguousarray(np.asarray(ins["L0_ffn_gate_w"]).T),
        "w1T": np.ascontiguousarray(np.asarray(ins["L0_ffn_w1"]).transpose(0, 2, 1)),
        "w3T": np.ascontiguousarray(np.asarray(ins["L0_ffn_w3"]).transpose(0, 2, 1)),
        "w2T": np.ascontiguousarray(np.asarray(ins["L0_ffn_w2"]).transpose(0, 2, 1)),
        "ws1T": np.ascontiguousarray(np.asarray(ins["L0_ffn_ws1"]).T),
        "ws3T": np.ascontiguousarray(np.asarray(ins["L0_ffn_ws3"]).T),
        "ws2T": np.ascontiguousarray(np.asarray(ins["L0_ffn_ws2"]).T),
        "bs1": np.asarray(ins["L0_ffn_bs1"]),
        "bs3": np.asarray(ins["L0_ffn_bs3"]),
        "bs2": np.asarray(ins["L0_ffn_bs2"]),
    }


def decl_l0_ffn_weights(nc):
    dts = {}
    for nm, sh in (("gwT", (1024, 8)), ("L0_ffn_gate_b", (8,)),
                   ("w1T", (8, 1024, 1024)), ("w3T", (8, 1024, 1024)), ("w2T", (8, 1024, 1024)),
                   ("L0_ffn_b1", (8, 1024)), ("L0_ffn_b3", (8, 1024)), ("L0_ffn_b2", (8, 1024)),
                   ("ws1T", (1024, 2048)), ("ws3T", (1024, 2048)), ("ws2T", (2048, 1024)),
                   ("bs1", (2048,)), ("bs3", (2048,)), ("bs2", (1024,))):
        dts[nm] = nc.dram_tensor(nm, list(sh), F32, kind="ExternalInput")
    return dts


def decl_consts(nc):
    import ml_dtypes
    dts = {}
    for k, v in host_constants().items():
        dt = BF16 if v.dtype == ml_dtypes.bfloat16 else F32
        dts[k] = nc.dram_tensor(k, list(v.shape), dt, kind="ExternalInput")
    return dts



def emit_l1_attn(nc, S, C, stack, xin, xout, W):
    NH = 16
    with contextlib.ExitStack() as st:
        xn = alloc_chunks(nc, st, "a1_xn", NCH, [128, T], BF16)
        emit_ln(nc, S, C, st, xin, xn, BF16)

        q_fm = alloc_chunks(nc, st, "a1_q", NCH, [128, T], BF16)
        out_pre = alloc_chunks(nc, st, "a1_op", NCH, [128, T], BF16)
        s1_sb = st.enter_context(nc.sbuf_tensor("a1_s1", [64, 1024], F32))
        s2_sb = st.enter_context(nc.sbuf_tensor("a1_s2", [64, 1024], F32))
        v1m = st.enter_context(nc.sbuf_tensor("a1_v1m", [1, 1024], F32))
        v2m = st.enter_context(nc.sbuf_tensor("a1_v2m", [1, 1024], F32))

        def load_wT(buf, key, add_ident):
            for r in range(NCH):
                S.dma("qg", buf[r][:], W[key].ap()[r * 128:(r + 1) * 128, :])
            if add_ident:
                for r in range(NCH):
                    S.tt(buf[r][:, r * 128:(r + 1) * 128], buf[r][:, r * 128:(r + 1) * 128],
                         C.ident_bf[:], ALU.add)

        def load_bias_col(sk, name, key):
            bc = sk.enter_context(nc.sbuf_tensor(name, [128, 8], F32))
            for c in range(NCH):
                S.dma("qs", bc[:, c], W[key].ap()[c * 128:(c + 1) * 128])
            return bc

        def load_bias_row(sk, name, key):
            br = sk.enter_context(nc.sbuf_tensor(name, [1, 1024], F32))
            bb = sk.enter_context(nc.sbuf_tensor(name + "b", [1, 1024], BF16))
            S.dma("qs", br[0:1, :], W[key].ap()[None, :])
            S.scopy(bb[:], br[:])
            return bb

        S.force_raw = False
        with contextlib.ExitStack() as qs_:
            wbuf = alloc_chunks(nc, qs_, "a1_w", NCH, [128, 1024], BF16)
            wvbuf = alloc_chunks(nc, qs_, "a1_wv", NCH, [128, 1024], BF16)
            bq_col = load_bias_col(qs_, "a1_bq", "L1_attn_bq")
            bk1_row = load_bias_row(qs_, "a1_bk1", "L1_attn_bk1")
            bk2_row = load_bias_row(qs_, "a1_bk2", "L1_attn_bk2")
            bv1_row = load_bias_row(qs_, "a1_bv1", "L1_attn_bv1")
            bv2_row = load_bias_row(qs_, "a1_bv2", "L1_attn_bv2")
            ps_p = [qs_.enter_context(nc.psum_tensor(f"a1_pp{i}", [128, 512], F32)) for i in range(2)]
            load_wT(wbuf, "WqT", True)
            for c in range(NCH):
                for t4 in range(4):
                    pp = ps_p[t4 % 2]
                    for m in range(NCH):
                        S.mm(pp[:, 0:512], wbuf[m][:, c * 128:(c + 1) * 128],
                             xn[m][:, t4 * 512:(t4 + 1) * 512],
                             start=(m == 0), stop=(m == NCH - 1))
                    S.ts(q_fm[c][:, t4 * 512:(t4 + 1) * 512], pp[:, 0:512],
                         bq_col[:, c:c + 1], None, ALU.add)

            kt = [qs_.enter_context(nc.sbuf_tensor(f"a1_kt{i}", [128, 384], BF16)) for i in range(2)]
            vt = [qs_.enter_context(nc.sbuf_tensor(f"a1_vt{i}", [128, 384], BF16)) for i in range(2)]
            ps_h = [qs_.enter_context(nc.psum_tensor(f"a1_psh{j}", [64, 64], F32)) for j in range(6)]
            mxn = qs_.enter_context(nc.sbuf_tensor("a1_mxn", [128, 8], F32))
            mxnb = qs_.enter_context(nc.sbuf_tensor("a1_mxnb", [128, 8], BF16))
            for c in range(NCH):
                S.red(mxn[:, c:c + 1], xn[c][:], ALU.add)
            S.ts(mxn[:], mxn[:], 1.0 / T, None, ALU.mult)
            S.scopy(mxnb[:], mxn[:])
            sets = [(0, 6), (6, 12), (12, 16)]
            for (wk, wv, bk_row, bv_row, s_sb, vm) in (
                    ("Wk1T", "Wv1T", bk1_row, bv1_row, s1_sb, v1m),
                    ("Wk2T", "Wv2T", bk2_row, bv2_row, s2_sb, v2m)):
                load_wT(wbuf, wk, True)
                load_wT(wvbuf, wv, True)
                for half in range(2):
                    pm = ps_p[half]
                    for r in range(NCH):
                        S.mm(pm[0:1, 0:512], mxnb[:, r:r + 1],
                             wvbuf[r][:, half * 512:(half + 1) * 512],
                             start=(r == 0), stop=False)
                    S.mm(pm[0:1, 0:512], C.ones_row_bf[0:1, 0:1],
                         bv_row[0:1, half * 512:(half + 1) * 512], start=False, stop=True)
                    S.copy("vector", vm[0:1, half * 512:(half + 1) * 512], pm[0:1, 0:512])
                for (h0, h1) in sets:
                    nh = h1 - h0
                    cw = nh * 64
                    for tc in range(16):
                        kk = kt[tc % 2]
                        vv = vt[tc % 2]
                        for (buf, brow, dst) in ((wbuf, bk_row, kk), (wvbuf, bv_row, vv)):
                            pp = ps_p[0 if dst is kk else 1]
                            for m in range(NCH):
                                S.mm(pp[:, 0:cw], xn[m][:, tc * 128:(tc + 1) * 128],
                                     buf[m][:, h0 * 64:h0 * 64 + cw],
                                     start=(m == 0), stop=False)
                            S.mm(pp[:, 0:cw], C.ones_row_bf[0:1, 0:128],
                                 brow[0:1, h0 * 64:h0 * 64 + cw], start=False, stop=True)
                            S.scopy(dst[:, 0:cw], pp[:, 0:cw])
                        for j in range(nh):
                            S.mm(ps_h[j][:], kk[:, j * 64:(j + 1) * 64],
                                 vv[:, j * 64:(j + 1) * 64],
                                 start=(tc == 0), stop=(tc == 15))
                    for j in range(nh):
                        S.ts(s_sb[:, (h0 + j) * 64:(h0 + j + 1) * 64], ps_h[j][:],
                             1.0 / T, None, ALU.mult)
            S.barrier()

        with contextlib.ExitStack() as cs_:
            gcols = cs_.enter_context(nc.sbuf_tensor("a1_gc", [64, 64], F32))
            grow = cs_.enter_context(nc.sbuf_tensor("a1_gr", [1, 64], F32))
            sc_bf = [cs_.enter_context(nc.sbuf_tensor(f"a1_sc{p}", [128, 64], BF16))
                     for p in range(NH // 2)]
            y2_bf = cs_.enter_context(nc.sbuf_tensor("a1_y2", [1, 1024], BF16))
            y2f = cs_.enter_context(nc.sbuf_tensor("a1_y2f", [1, 1024], F32))
            t1 = cs_.enter_context(nc.sbuf_tensor("a1_t1", [64, 64], F32))
            t2 = cs_.enter_context(nc.sbuf_tensor("a1_t2", [64, 64], F32))
            t3 = cs_.enter_context(nc.sbuf_tensor("a1_t3", [64, 64], F32))
            bcv = cs_.enter_context(nc.sbuf_tensor("a1_bcv", [64, 64], F32))
            ps_bc = cs_.enter_context(nc.psum_tensor("a1_pbc", [64, 64], F32))
            ps_r = cs_.enter_context(nc.psum_tensor("a1_pr", [1, 64], F32))
            ps_g = cs_.enter_context(nc.psum_tensor("a1_pg", [64, 64], F32))
            ps_y = [cs_.enter_context(nc.psum_tensor(f"a1_py{i}", [64, 512], F32)) for i in range(2)]

            S.dma("qs", grow[0:1, 0:16], W["L1_attn_g1"].ap()[None, :])
            S.dma("qs", grow[0:1, 16:32], W["L1_attn_g2"].ap()[None, :])
            S.dma("qs", grow[0:1, 32:48], W["L1_attn_g3"].ap()[None, :])
            S.dma("qs", grow[0:1, 48:64], W["L1_attn_g4"].ap()[None, :])
            S.mm(ps_g[:, 0:64], C.ones_row_f[0:1, 0:64], grow[0:1, 0:64])
            S.copy("vector", gcols[:], ps_g[:, 0:64])

            for h in range(NH):
                hs = slice(h * 64, (h + 1) * 64)
                S.mm(ps_bc[:], C.ones_row_f[0:1, 0:64], v2m[0:1, hs])
                S.copy("vector", bcv[:], ps_bc[:])
                S.tt(t1[:], s1_sb[:, hs], bcv[:], ALU.mult)
                S.ts(t1[:], t1[:], gcols[:, h:h + 1], None, ALU.mult)
                S.mm(ps_bc[:], C.ones_row_f[0:1, 0:64], v1m[0:1, hs])
                S.copy("vector", bcv[:], ps_bc[:])
                S.tt(t2[:], s2_sb[:, hs], bcv[:], ALU.mult)
                S.ts(t2[:], t2[:], gcols[:, 32 + h:32 + h + 1], None, ALU.mult)
                S.tt(t3[:], s1_sb[:, hs], s2_sb[:, hs], ALU.mult)
                S.mm(ps_r[:], C.ones_col_f[0:64, 0:1], t3[:])
                S.ts(y2f[0:1, hs], ps_r[:], gcols[0:1, 16 + h:16 + h + 1], None, ALU.mult)
                S.ts(t3[:], t3[:], gcols[:, 48 + h:48 + h + 1], None, ALU.mult)
                S.tt(t1[:], t1[:], t2[:], ALU.add)
                S.tt(sc_bf[h // 2][(h % 2) * 64:(h % 2) * 64 + 64, :], t1[:], t3[:], ALU.add)
            S.scopy(y2_bf[:], y2f[:])

            for h in range(NH):
                hs = slice(h * 64, (h + 1) * 64)
                for t4 in range(4):
                    py = ps_y[t4 % 2]
                    S.mm(py[:, 0:512], sc_bf[h // 2][(h % 2) * 64:(h % 2) * 64 + 64, :],
                         q_fm[h // 2][(h % 2) * 64:(h % 2) * 64 + 64, t4 * 512:(t4 + 1) * 512],
                         start=True, stop=False)
                    S.mm(py[:, 0:512], y2_bf[0:1, hs], C.ones_row_bf[0:1, 0:512],
                         start=False, stop=True)
                    S.copy("vector",
                           out_pre[h // 2][(h % 2) * 64:(h % 2) * 64 + 64, t4 * 512:(t4 + 1) * 512],
                           py[:, 0:512])
            S.barrier()

        with contextlib.ExitStack() as ls_:
            wbuf = alloc_chunks(nc, ls_, "a1_w2", NCH, [128, 1024], BF16)
            lng = load_bias_col(ls_, "a1_lng", "L1_attn_ln_g")
            lnb = load_bias_col(ls_, "a1_lnb", "L1_attn_ln_b")
            xn2 = alloc_chunks(nc, ls_, "a1_xn2", NCH, [128, T], BF16)
            sqq = [ls_.enter_context(nc.sbuf_tensor(f"a1_sq{i}", [128, 512], BF16)) for i in range(2)]
            tmp = [ls_.enter_context(nc.sbuf_tensor(f"a1_tm{i}", [128, 512], F32)) for i in range(2)]
            rowa = ls_.enter_context(nc.sbuf_tensor("a1_ra", [1, 512], F32))
            rowb = ls_.enter_context(nc.sbuf_tensor("a1_rb", [1, 512], F32))
            rowc = ls_.enter_context(nc.sbuf_tensor("a1_rc", [1, 512], F32))
            ps_s2 = ls_.enter_context(nc.psum_tensor("a1_ps2", [1, 512], F32))
            ps_q2 = ls_.enter_context(nc.psum_tensor("a1_pq2", [1, 512], F32))
            ps_a = ls_.enter_context(nc.psum_tensor("a1_pa2", [128, 512], F32))
            ps_b = ls_.enter_context(nc.psum_tensor("a1_pb2", [128, 512], F32))
            for t4 in range(4):
                tsl = slice(t4 * 512, (t4 + 1) * 512)
                for c in range(NCH):
                    s = sqq[c % 2]
                    S.act(s[:], out_pre[c][:, tsl], AF.Square)
                    S.mm(ps_s2[:], C.ones_col_bf[:], out_pre[c][:, tsl],
                         start=(c == 0), stop=(c == NCH - 1))
                    S.mm(ps_q2[:], C.ones_col_bf[:], s[:],
                         start=(c == 0), stop=(c == NCH - 1))
                S.ts(rowa[:], ps_s2[:], 1.0 / D, None, ALU.mult)
                S.ts(rowb[:], ps_q2[:], 1.0 / D, None, ALU.mult)
                S.tt(rowc[:], rowa[:], rowa[:], ALU.mult)
                S.tt(rowb[:], rowb[:], rowc[:], ALU.subtract)
                S.ts(rowb[:], rowb[:], 1e-5, None, ALU.add)
                S.rowpow(rowb[:], -0.5)
                S.stt(rowc[:], rowa[:], -1.0, rowb[:], ALU.mult, ALU.mult)
                S.mm(ps_a[:], C.ones_row_f[0:1, 0:128], rowb[:])
                S.mm(ps_b[:], C.ones_row_f[0:1, 0:128], rowc[:])
                for c in range(NCH):
                    tm = tmp[c % 2]
                    S.tt(tm[:], out_pre[c][:, tsl], ps_a[:], ALU.mult)
                    S.tt(tm[:], tm[:], ps_b[:], ALU.add)
                    S.act(xn2[c][:, tsl], tm[:], AF.Identity,
                          bias=lnb[:, c:c + 1], scale=lng[:, c:c + 1])
            S.barrier()

            bo_row = load_bias_row(ls_, "a1_bo", "L1_attn_bo")
            load_wT(wbuf, "WoT_1", False)
            xt = [ls_.enter_context(nc.sbuf_tensor(f"a1_xt{i}", [128, 512], F32)) for i in range(2)]
            xo = [ls_.enter_context(nc.sbuf_tensor(f"a1_xo{i}", [128, 512], F32)) for i in range(2)]
            for dm in range(NCH):
                for t4 in range(4):
                    pso = ps_a if (dm * 4 + t4) % 2 == 0 else ps_b
                    for ic in range(NCH):
                        S.mm(pso[:], wbuf[ic][:, dm * 128:(dm + 1) * 128],
                             xn2[ic][:, t4 * 512:(t4 + 1) * 512], start=(ic == 0), stop=False)
                    S.mm(pso[:], bo_row[0:1, dm * 128:(dm + 1) * 128],
                         C.ones_row_bf[0:1, 0:512], start=False, stop=True)
                    i2 = (dm * 4 + t4) % 2
                    S.dma("qs", xt[i2][:], xin[dm * 128:(dm + 1) * 128, t4 * 512:(t4 + 1) * 512])
                    S.tt(xo[i2][:], pso[:], xt[i2][:], ALU.add)
                    S.dma("qs", xout[dm * 128:(dm + 1) * 128, t4 * 512:(t4 + 1) * 512], xo[i2][:])
            S.force_raw = True
            S.barrier()


def host_l1_attn_inputs(ins):
    return {
        "WqT": np.ascontiguousarray(np.asarray(ins["L1_attn_Wq"]).T),
        "Wk1T": np.ascontiguousarray(np.asarray(ins["L1_attn_Wk1"]).T),
        "Wk2T": np.ascontiguousarray(np.asarray(ins["L1_attn_Wk2"]).T),
        "Wv1T": np.ascontiguousarray(np.asarray(ins["L1_attn_Wv1"]).T),
        "Wv2T": np.ascontiguousarray(np.asarray(ins["L1_attn_Wv2"]).T),
        "WoT_1": np.ascontiguousarray(np.asarray(ins["L1_attn_Wo"]).T),
    }


def decl_l1_attn_weights(nc):
    dts = {}
    for nm, sh in (("WqT", (1024, 1024)), ("Wk1T", (1024, 1024)), ("Wk2T", (1024, 1024)),
                   ("Wv1T", (1024, 1024)), ("Wv2T", (1024, 1024)), ("WoT_1", (1024, 1024)),
                   ("L1_attn_bq", (1024,)), ("L1_attn_bk1", (1024,)), ("L1_attn_bk2", (1024,)),
                   ("L1_attn_bv1", (1024,)), ("L1_attn_bv2", (1024,)), ("L1_attn_bo", (1024,)),
                   ("L1_attn_g1", (16,)), ("L1_attn_g2", (16,)), ("L1_attn_g3", (16,)),
                   ("L1_attn_g4", (16,)), ("L1_attn_ln_g", (1024,)), ("L1_attn_ln_b", (1024,))):
        dts[nm] = nc.dram_tensor(nm, list(sh), F32, kind="ExternalInput")
    return dts



def emit_l1_ffn(nc, S, C, stack, xin, xout, W):
    NHH = 8
    HID = 2730
    NIC = 22
    u_dram = nc.dram_tensor("f1_u_scr", [1024, T], F32).ap()
    with contextlib.ExitStack() as st:
        emit_ln(nc, S, C, st, xin, None, F32, out_dram=u_dram)

        rot2 = st.enter_context(nc.sbuf_tensor("f1_rot2", [128, 128], BF16))
        S.dma("qs", rot2[:], W["c_rot2"].ap())
        tabs = {nm: st.enter_context(nc.sbuf_tensor(f"f1_{nm}", [128, T], BF16))
                for nm in ("csq2", "snq2", "csk2", "snk2")}
        for nm in ("csq2", "snq2", "csk2", "snk2"):
            S.dma("qg", tabs[nm][:], W[nm].ap())

        S.force_raw = False
        for l in range(2):
          with contextlib.ExitStack() as lst:
            A = alloc_chunks(nc, lst, f"f1_A{l}", 24, [128, T], BF16)
            qk_fm = A[0:8]
            v_tok = [A[8 + tc // 2] for tc in range(16)]
            vcol = lambda tc, c0, c1: v_tok[tc][:, (tc % 2) * 1024 + c0:(tc % 2) * 1024 + c1]
            o_fm = A[16:20]
            hsh = A[16:24]
            with contextlib.ExitStack() as s1:
                P = [s1.enter_context(nc.psum_tensor(f"f1_P{i}", [128, 512], F32))
                     for i in range(8)]
                for c in range(NCH):
                    if c < 4:
                        S.dma("qg", hsh[c][:], u_dram[c * 128:(c + 1) * 128, :])
                    else:
                        S.memset(hsh[c][:, 0:1], 0.0)
                        S.dma("qg", hsh[c][:, 1:T], u_dram[c * 128:(c + 1) * 128, 0:T - 1])
                wq = alloc_chunks(nc, s1, "f1_wq", NCH, [128, 1544], BF16)
                for r in range(NCH):
                    S.dma("qg", wq[r][:], W["wqkv_aug"].ap()[l, r * 128:(r + 1) * 128, :])
                vone = s1.enter_context(nc.sbuf_tensor("f1_vone", [1, 520], F32))
                voneb = s1.enter_context(nc.sbuf_tensor("f1_voneb", [1, 520], BF16))
                S.dma("qs", vone[:], W["vones_row"].ap())
                S.scopy(voneb[:], vone[:])
                for c in range(NCH):
                    for t4 in range(4):
                        pp = P[t4 % 2]
                        for m in range(NCH):
                            S.mm(pp[:, 0:512], wq[m][:, c * 128:(c + 1) * 128],
                                 hsh[m][:, t4 * 512:(t4 + 1) * 512],
                                 start=(m == 0), stop=(m == NCH - 1))
                        S.copy("vector", qk_fm[c][:, t4 * 512:(t4 + 1) * 512], pp[:, 0:512])
                for tc in range(16):
                    pp = P[tc % 2]
                    for half in range(2):
                        for m in range(NCH):
                            S.mm(pp[:, 0:260], hsh[m][:, tc * 128:(tc + 1) * 128],
                                 wq[m][:, 1024 + half * 260:1024 + (half + 1) * 260],
                                 start=(m == 0), stop=False)
                        S.mm(pp[:, 0:260], C.ones_row_bf[0:1, 0:128],
                             voneb[0:1, half * 260:(half + 1) * 260], start=False, stop=True)
                        S.copy("vector", vcol(tc, half * 260, (half + 1) * 260), pp[:, 0:260])

                qs2 = s1.enter_context(nc.sbuf_tensor("f1_qs2", [128, 1], F32))
                ks2 = s1.enter_context(nc.sbuf_tensor("f1_ks2", [128, 1], F32))
                for b0 in (0, 64):
                    S.dma("qs", qs2[b0:b0 + 64, 0], W["L1_ffn_q_scale"].ap()[l, :])
                    S.dma("qs", ks2[b0:b0 + 64, 0], W["L1_ffn_k_scale"].ap()[l, :])
                S.ts(qs2[:], qs2[:], 8.0, None, ALU.mult)
                sq128 = [s1.enter_context(nc.sbuf_tensor(f"f1_sq{i}", [128, 512], BF16)) for i in range(2)]
                nrA = s1.enter_context(nc.sbuf_tensor("f1_nrA", [1, T], F32))
                nrB = s1.enter_context(nc.sbuf_tensor("f1_nrB", [1, T], F32))
                bcv = [s1.enter_context(nc.sbuf_tensor(f"f1_bcv{i}", [128, 512], F32)) for i in range(2)]
                tt1 = [s1.enter_context(nc.sbuf_tensor(f"f1_tt1{i}", [128, 512], BF16)) for i in range(2)]
                tt2 = [s1.enter_context(nc.sbuf_tensor(f"f1_tt2{i}", [128, 512], BF16)) for i in range(2)]
                for c in (0, 4, 1, 5, 2, 6, 3, 7):
                    isq = c < 4
                    scol = qs2 if isq else ks2
                    cst = tabs["csq2" if isq else "csk2"]
                    snt = tabs["snq2" if isq else "snk2"]
                    ch = qk_fm[c]
                    for t4 in range(4):
                        S.tt(sq128[t4 % 2][:], ch[:, t4 * 512:(t4 + 1) * 512],
                             ch[:, t4 * 512:(t4 + 1) * 512], ALU.mult, eng="gpsimd")
                        for (b0, nr) in ((0, nrA), (64, nrB)):
                            S.mm(P[2][0:1, 0:512], C.ones_col_bf[b0:b0 + 64, :],
                                 sq128[t4 % 2][b0:b0 + 64, :])
                            S.copy("vector", nr[:, t4 * 512:(t4 + 1) * 512], P[2][0:1, 0:512])
                    for nr in (nrA, nrB):
                        S.ts(nr[:], nr[:], 1e-24, None, ALU.max)
                        S.rowpow(nr[:], -0.5)
                    for t4 in range(4):
                        tsl = slice(t4 * 512, (t4 + 1) * 512)
                        S.mm(P[3][0:64, 0:512], C.ones_row_f[0:1, 0:64], nrA[0:1, tsl])
                        S.mm(P[3][64:128, 0:512], C.ones_row_f[0:1, 0:64], nrB[0:1, tsl])
                        ib = t4 % 2
                        S.copy("vector", bcv[ib][:], P[3][:, 0:512])
                        S.stt(ch[:, tsl], ch[:, tsl], scol[:], bcv[ib][:], ALU.mult, ALU.mult)
                        S.mm(P[4 + ib][:, 0:512], rot2[:], ch[:, tsl])
                        S.tt(tt1[ib][:], ch[:, tsl], cst[:, tsl], ALU.mult)
                        S.tt(tt2[ib][:], P[4 + ib][:, 0:512], snt[:, tsl], ALU.mult)
                        S.tt(ch[:, tsl], tt1[ib][:], tt2[ib][:], ALU.add)

                q_aug = s1.enter_context(nc.sbuf_tensor("f1_qa", [65, T], BF16))
                k_aug = s1.enter_context(nc.sbuf_tensor("f1_ka", [65, T], BF16))
                mnb = [s1.enter_context(nc.sbuf_tensor(f"f1_mnb{i}", [128, 1], BF16))
                       for i in range(8)]
                mcols = [s1.enter_context(nc.sbuf_tensor(f"f1_mcol{i}", [128, 1], F32))
                         for i in range(4)]
                mtl = [s1.enter_context(nc.sbuf_tensor(f"f1_mtl{i}", [128, 1], F32))
                       for i in range(4)]
                prb = [s1.enter_context(nc.sbuf_tensor(f"f1_prb{i}", [128, 512], BF16)) for i in range(3)]
                rrec = s1.enter_context(nc.sbuf_tensor("f1_rr", [1, 512], F32))
                bcs = s1.enter_context(nc.sbuf_tensor("f1_bcs", [64, 512], F32))
                ps_s = [P[0], P[1], P[2]]
                ps_avs = [P[6], P[7]]
                for h in range(NHH):
                    qch, b0 = qk_fm[h // 2], (h % 2) * 64
                    kch = qk_fm[4 + h // 2]
                    S.copy("gpsimd", q_aug[0:64, :], qch[b0:b0 + 64, :])
                    S.copy("gpsimd", k_aug[0:64, :], kch[b0:b0 + 64, :])
                    S.memset(k_aug[64:65, :], 1.0)
                    for j in range(4):
                        ps_av = ps_avs[j % 2]
                        qsl = slice(j * 512, (j + 1) * 512)
                        for qc4 in range(4):
                            qq = q_aug[0:64, j * 512 + qc4 * 128: j * 512 + (qc4 + 1) * 128]
                            mcol = mcols[qc4]
                            pss1 = ps_s[qc4 % 3]
                            if j == 0:
                                kc0, kw = 0, (qc4 + 1) * 128
                                S.mm(pss1[:, 0:kw], qq, k_aug[0:64, kc0:kc0 + kw])
                                S.tt(pss1[:, qc4 * 128:kw], pss1[:, qc4 * 128:kw],
                                     C.t_up[:], ALU.add)
                                S.red(mcol[:], pss1[:, 0:kw], ALU.max)
                            else:
                                kc0 = (j - 1) * 512 + qc4 * 128
                                S.mm(pss1[:, 0:512], qq, k_aug[0:64, kc0:kc0 + 512])
                                S.tt(pss1[:, 0:128], pss1[:, 0:128], C.t_lo[:], ALU.add)
                                S.mm(P[3][:, qc4 * 128:(qc4 + 1) * 128], qq,
                                     k_aug[0:64, kc0 + 512:kc0 + 640])
                                S.tt(P[3][:, qc4 * 128:(qc4 + 1) * 128],
                                     P[3][:, qc4 * 128:(qc4 + 1) * 128], C.t_up[:], ALU.add)
                                S.red(mcol[:], pss1[:, 0:512], ALU.max)
                                S.red(mtl[qc4][:], P[3][:, qc4 * 128:(qc4 + 1) * 128], ALU.max)
                                S.tt(mcol[:], mcol[:], mtl[qc4][:], ALU.max)
                            mb = mnb[(j % 2) * 4 + qc4]
                            S.ts(mb[:], mcol[:], -1.0, None, ALU.mult)
                            S.dma("qs", q_aug[64:65, j * 512 + qc4 * 128: j * 512 + (qc4 + 1) * 128],
                                  mb[:])
                        if j == 0:
                            kslist = [(ks, ks * 128, ks * 128, 512 - ks * 128, "lo")
                                      for ks in range(4)]
                        else:
                            kslist = []
                            for ks in range(8):
                                kc = (j - 1) * 512 + ks * 128
                                if ks <= 3:
                                    kslist.append((kc // 128, kc, 0, (ks + 1) * 128, "up"))
                                else:
                                    kslist.append((kc // 128, kc, (ks - 4) * 128, 512 - (ks - 4) * 128, "lo"))
                        nck = len(kslist)

                        def emit_score2(i):
                            vc, kc, n0, nw, mty = kslist[i]
                            pss = ps_s[i % 3]
                            S.mm(pss[:, n0:n0 + nw], k_aug[:, kc:kc + 128],
                                 q_aug[:, j * 512 + n0: j * 512 + n0 + nw])
                            if mty == "up":
                                S.tt(pss[:, n0 + nw - 128:n0 + nw], pss[:, n0 + nw - 128:n0 + nw],
                                     C.t_up[:], ALU.add)
                            else:
                                S.tt(pss[:, n0:n0 + 128], pss[:, n0:n0 + 128], C.t_lo[:], ALU.add)
                            pr = prb[i % 3]
                            if n0 > 0:
                                S.memset(pr[:, 0:n0], 0.0)
                            if n0 + nw < 512:
                                S.memset(pr[:, n0 + nw:512], 0.0)
                            S.act(pr[:, n0:n0 + nw], pss[:, n0:n0 + nw], AF.Exp)

                        def emit_av2(i):
                            vc = kslist[i][0]
                            S.mm(ps_av[0:65, 0:512], vcol(vc, h * 65, (h + 1) * 65), prb[i % 3][:],
                                 start=(i == 0), stop=(i == nck - 1))

                        for i in range(nck):
                            emit_score2(i)
                            if i >= 2:
                                emit_av2(i - 2)
                        emit_av2(nck - 2)
                        emit_av2(nck - 1)
                        S.copy("vector", rrec[:], ps_av[64:65, 0:512])
                        S.rowpow(rrec[:], -1.0)
                        S.mm(P[4][0:64, 0:512], C.ones_row_f[0:1, 0:64], rrec[0:1, :])
                        S.copy("vector", bcs[:], P[4][0:64, 0:512])
                        S.tt(o_fm[h // 2][b0:b0 + 64, qsl], ps_av[0:64, 0:512], bcs[:], ALU.mult)

                wo = alloc_chunks(nc, s1, "f1_wo", 4, [128, 1024], BF16)
                for r in range(4):
                    S.dma("qg", wo[r][:], W["L1_ffn_wo"].ap()[l, r * 128:(r + 1) * 128, :])
                ut = [s1.enter_context(nc.sbuf_tensor(f"f1_ut{i}", [128, 512], F32)) for i in range(2)]
                uo = [s1.enter_context(nc.sbuf_tensor(f"f1_uo{i}", [128, 512], F32)) for i in range(2)]
                for dm in range(NCH):
                    for t4 in range(4):
                        i2 = (dm * 4 + t4) % 2
                        py = P[i2]
                        tsl = slice(t4 * 512, (t4 + 1) * 512)
                        for ic in range(4):
                            S.mm(py[:, 0:512], wo[ic][:, dm * 128:(dm + 1) * 128],
                                 o_fm[ic][:, tsl], start=(ic == 0), stop=(ic == 3))
                        S.dma("qs", ut[i2][:], u_dram[dm * 128:(dm + 1) * 128, tsl])
                        S.tt(uo[i2][:], py[:, 0:512], ut[i2][:], ALU.add)
                        S.dma("qs", u_dram[dm * 128:(dm + 1) * 128, tsl], uo[i2][:])
                S.barrier()

            with contextlib.ExitStack() as s2:
                P = [s2.enter_context(nc.psum_tensor(f"f1_Q{i}", [128, 512], F32))
                     for i in range(8)]
                xnf = alloc_chunks(nc, s2, f"f1_xnf{l}", NCH, [128, T], BF16)
                ush = A[8:16]
                for c in range(NCH):
                    if c < 4:
                        S.dma("qg", ush[c][:], u_dram[c * 128:(c + 1) * 128, :])
                    else:
                        S.memset(ush[c][:, 0:1], 0.0)
                        S.dma("qg", ush[c][:, 1:T], u_dram[c * 128:(c + 1) * 128, 0:T - 1])
                lng = s2.enter_context(nc.sbuf_tensor("f1_lng", [128, 8], F32))
                lnb = s2.enter_context(nc.sbuf_tensor("f1_lnb", [128, 8], F32))
                for c in range(NCH):
                    S.dma("qs", lng[:, c], W["L1_ffn_ln_g"].ap()[l, c * 128:(c + 1) * 128])
                    S.dma("qs", lnb[:, c], W["L1_ffn_ln_b"].ap()[l, c * 128:(c + 1) * 128])
                sqq = [s2.enter_context(nc.sbuf_tensor(f"f1_sqq{i}", [128, 512], BF16)) for i in range(2)]
                tmp = [s2.enter_context(nc.sbuf_tensor(f"f1_tmq{i}", [128, 512], F32)) for i in range(2)]
                rows = [[s2.enter_context(nc.sbuf_tensor(f"f1_r{nm}{i}", [1, 512], F32))
                         for i in range(2)] for nm in "abc"]
                for t4 in range(4):
                    rowa, rowb, rowc = rows[0][t4 % 2], rows[1][t4 % 2], rows[2][t4 % 2]
                    ps_s2, ps_q2 = P[4 + t4 % 2], P[6 + t4 % 2]
                    ps_a, ps_b2 = P[t4 % 2], P[2 + t4 % 2]
                    tsl = slice(t4 * 512, (t4 + 1) * 512)
                    for c in range(NCH):
                        s_ = sqq[c % 2]
                        S.tt(s_[:], ush[c][:, tsl], ush[c][:, tsl], ALU.mult, eng="gpsimd")
                        S.mm(ps_s2[0:1, 0:512], C.ones_col_bf[:], ush[c][:, tsl],
                             start=(c == 0), stop=(c == NCH - 1))
                        S.mm(ps_q2[0:1, 0:512], C.ones_col_bf[:], s_[:],
                             start=(c == 0), stop=(c == NCH - 1))
                    S.ts(rowa[:], ps_s2[0:1, 0:512], 1.0 / D, None, ALU.mult)
                    S.ts(rowb[:], ps_q2[0:1, 0:512], 1.0 / D, None, ALU.mult)
                    S.tt(rowc[:], rowa[:], rowa[:], ALU.mult)
                    S.tt(rowb[:], rowb[:], rowc[:], ALU.subtract)
                    S.ts(rowb[:], rowb[:], 1e-5, None, ALU.add)
                    S.rowpow(rowb[:], -0.5)
                    S.stt(rowc[:], rowa[:], -1.0, rowb[:], ALU.mult, ALU.mult)
                    S.mm(ps_a[:, 0:512], C.ones_row_f[0:1, 0:128], rowb[:])
                    S.mm(ps_b2[:, 0:512], C.ones_row_f[0:1, 0:128], rowc[:])
                    for c in range(NCH):
                        tm = tmp[c % 2]
                        S.tt(tm[:], ush[c][:, tsl], ps_a[:, 0:512], ALU.mult)
                        S.tt(tm[:], tm[:], ps_b2[:, 0:512], ALU.add)
                        S.act(xnf[c][:, tsl], tm[:], AF.Identity,
                              bias=lnb[:, c:c + 1], scale=lng[:, c:c + 1])

                hh = A[0:22]
                gsc = contextlib.ExitStack()
                w1ag = [alloc_chunks(nc, gsc, f"f1_w1a{i}", NCH, [128, 512], BF16) for i in range(2)]
                w1gg = [alloc_chunks(nc, gsc, f"f1_w1g{i}", NCH, [128, 512], BF16) for i in range(2)]
                b1aA = gsc.enter_context(nc.sbuf_tensor("f1_b1aA", [128, NIC], F32))
                b1gA = gsc.enter_context(nc.sbuf_tensor("f1_b1gA", [128, NIC], F32))
                for ic in range(NIC):
                    cw = min(128, HID - ic * 128)
                    S.dma("qs", b1aA[0:cw, ic], W["L1_ffn_b1"].ap()[l, ic * 128:ic * 128 + cw])
                    S.dma("qs", b1gA[0:cw, ic], W["L1_ffn_b1"].ap()[l, HID + ic * 128:HID + ic * 128 + cw])
                aa = [gsc.enter_context(nc.sbuf_tensor(f"f1_aa{i}", [128, 512], BF16)) for i in range(2)]
                gg = [gsc.enter_context(nc.sbuf_tensor(f"f1_gg{i}", [128, 512], BF16)) for i in range(2)]
                NG = (NIC + 3) // 4
                for icg in range(NG):
                    gc0 = icg * 512
                    gw = min(512, HID - gc0)
                    wa, wg = w1ag[icg % 2], w1gg[icg % 2]
                    for r in range(NCH):
                        S.dma("qg", wa[r][:, 0:gw],
                              W["L1_ffn_w1"].ap()[l, r * 128:(r + 1) * 128, gc0:gc0 + gw])
                        S.dma("qg", wg[r][:, 0:gw],
                              W["L1_ffn_w1"].ap()[l, r * 128:(r + 1) * 128, HID + gc0:HID + gc0 + gw])
                    for ici in range((gw + 127) // 128):
                        ic = icg * 4 + ici
                        cw = min(128, HID - ic * 128)
                        for t4 in range(4):
                            tsl = slice(t4 * 512, (t4 + 1) * 512)
                            pa, pg = P[(t4 % 2) * 2], P[(t4 % 2) * 2 + 1]
                            for m in range(NCH):
                                S.mm(pa[0:cw, 0:512], wa[m][:, ici * 128:ici * 128 + cw],
                                     xnf[m][:, tsl], start=(m == 0), stop=(m == NCH - 1))
                            for m in range(NCH):
                                S.mm(pg[0:cw, 0:512], wg[m][:, ici * 128:ici * 128 + cw],
                                     xnf[m][:, tsl], start=(m == 0), stop=(m == NCH - 1))
                            S.ts(aa[t4 % 2][0:cw, :], pa[0:cw, 0:512],
                                 b1aA[0:cw, ic:ic + 1], None, ALU.add)
                            S.act(gg[t4 % 2][0:cw, :], pg[0:cw, 0:512], AF.Gelu,
                                  bias=b1gA[0:cw, ic:ic + 1])
                            S.tt(hh[ic][0:cw, tsl], aa[t4 % 2][0:cw, :], gg[t4 % 2][0:cw, :],
                                 ALU.mult)

                S.barrier()
                gsc.close()
                w2m = [s2.enter_context(nc.sbuf_tensor(f"f1_w2m{i}", [128, 2688], BF16)) for i in range(2)]
                w2t = [s2.enter_context(nc.sbuf_tensor(f"f1_w2t{i}", [42, 128], BF16)) for i in range(2)]
                b2r = s2.enter_context(nc.sbuf_tensor("f1_b2r", [1, 1024], F32))
                b2b = s2.enter_context(nc.sbuf_tensor("f1_b2b", [1, 1024], BF16))
                S.dma("qs", b2r[0:1, :], W["L1_ffn_b2"].ap()[l:l + 1, :])
                S.scopy(b2b[:], b2r[:])
                ut2 = [s2.enter_context(nc.sbuf_tensor(f"f1_ut2{i}", [128, 512], F32)) for i in range(2)]
                uo2 = [s2.enter_context(nc.sbuf_tensor(f"f1_uo2{i}", [128, 512], F32)) for i in range(2)]
                for dm in range(NCH):
                    wm, wt = w2m[dm % 2], w2t[dm % 2]
                    S.dma("qg", wm[:].rearrange("p (i c) -> p i c", c=128),
                          W["L1_ffn_w2"].ap()[l, 0:2688, dm * 128:(dm + 1) * 128]
                          .rearrange("(i p) c -> p i c", p=128))
                    S.dma("qg", wt[:], W["L1_ffn_w2"].ap()[l, 2688:2730, dm * 128:(dm + 1) * 128])
                    for t4 in range(4):
                        i2 = (dm * 4 + t4) % 2
                        py = P[4 + i2]
                        tsl = slice(t4 * 512, (t4 + 1) * 512)
                        for ic in range(21):
                            S.mm(py[:, 0:512], wm[:, ic * 128:(ic + 1) * 128], hh[ic][:, tsl],
                                 start=(ic == 0), stop=False)
                        S.mm(py[:, 0:512], wt[0:42, :], hh[21][0:42, tsl],
                             start=False, stop=False)
                        S.mm(py[:, 0:512], b2b[0:1, dm * 128:(dm + 1) * 128],
                             C.ones_row_bf[0:1, 0:512], start=False, stop=True)
                        S.dma("qs", ut2[i2][:], u_dram[dm * 128:(dm + 1) * 128, tsl])
                        S.tt(uo2[i2][:], py[:, 0:512], ut2[i2][:], ALU.add)
                        S.dma("qs", u_dram[dm * 128:(dm + 1) * 128, tsl], uo2[i2][:])
                S.barrier()

        S.force_raw = True
        with contextlib.ExitStack() as os_:
            xt = [os_.enter_context(nc.sbuf_tensor(f"f1_xt{i}", [128, T], F32)) for i in range(2)]
            xu = [os_.enter_context(nc.sbuf_tensor(f"f1_xu{i}", [128, T], F32)) for i in range(2)]
            xo = [os_.enter_context(nc.sbuf_tensor(f"f1_xo{i}", [128, T], F32)) for i in range(2)]
            for c in range(NCH):
                i2 = c % 2
                S.dma("qs", xt[i2][:], xin[c * 128:(c + 1) * 128, :])
                S.dma("qs", xu[i2][:], u_dram[c * 128:(c + 1) * 128, :])
                S.tt(xo[i2][:], xu[i2][:], xt[i2][:], ALU.add)
                S.dma("qs", xout[c * 128:(c + 1) * 128, :], xo[i2][:])
            S.barrier()


def host_l1_ffn_inputs(ins):
    wqkv = np.asarray(ins["L1_ffn_wqkv"])
    aug = np.zeros((2, 1024, 1544), np.float32)
    aug[:, :, 0:1024] = wqkv[:, :, 0:1024]
    for h in range(8):
        aug[:, :, 1024 + h * 65:1024 + h * 65 + 64] = wqkv[:, :, 1024 + h * 64:1024 + (h + 1) * 64]
    vones = np.zeros((1, 520), np.float32)
    for h in range(8):
        vones[0, h * 65 + 64] = 1.0
    DIM_HEAD, WINDOW, n = 64, 512, 2048
    half_idx = np.arange(0, DIM_HEAD, 2, dtype=np.float32)
    inv_freq = 1.0 / 10000.0 ** (half_idx / DIM_HEAD)
    t = np.arange(n, dtype=np.float32)
    freqs = t[:, None] * inv_freq[None, :]
    freqs = np.concatenate([freqs, freqs], axis=-1)
    base = (half_idx + 0.4 * DIM_HEAD) / (1.4 * DIM_HEAD)
    power = (t - n // 2) / float(WINDOW // 2)
    scale = base[None, :] ** power[:, None]
    scale = np.concatenate([scale, scale], axis=-1)
    cos, sin = np.cos(freqs), np.sin(freqs)
    csq = (cos * scale).T.astype(np.float32)
    snq = (sin * scale).T.astype(np.float32)
    csk = (cos / scale).T.astype(np.float32)
    snk = (sin / scale).T.astype(np.float32)
    stack2 = lambda a: np.ascontiguousarray(np.concatenate([a, a], axis=0))
    import ml_dtypes
    R = np.zeros((64, 64), np.float32)
    for i in range(32):
        R[i, 32 + i] = -1.0
        R[32 + i, i] = 1.0
    RT = R.T
    rot2 = np.zeros((128, 128), np.float32)
    rot2[0:64, 0:64] = RT
    rot2[64:128, 64:128] = RT
    return {
        "wqkv_aug": aug, "vones_row": vones,
        "csq2": stack2(csq), "snq2": stack2(snq), "csk2": stack2(csk), "snk2": stack2(snk),
        "c_rot2": rot2.astype(ml_dtypes.bfloat16),
    }


def decl_l1_ffn_weights(nc):
    dts = {}
    for nm, sh, dt in (("wqkv_aug", (2, 1024, 1544), F32), ("vones_row", (1, 520), F32),
                       ("csq2", (128, 2048), F32), ("snq2", (128, 2048), F32),
                       ("csk2", (128, 2048), F32), ("snk2", (128, 2048), F32),
                       ("c_rot2", (128, 128), BF16),
                       ("L1_ffn_q_scale", (2, 64), F32), ("L1_ffn_k_scale", (2, 64), F32),
                       ("L1_ffn_wo", (2, 512, 1024), F32),
                       ("L1_ffn_ln_g", (2, 1024), F32), ("L1_ffn_ln_b", (2, 1024), F32),
                       ("L1_ffn_w1", (2, 1024, 5460), F32), ("L1_ffn_b1", (2, 5460), F32),
                       ("L1_ffn_w2", (2, 2730, 1024), F32), ("L1_ffn_b2", (2, 1024), F32)):
        dts[nm] = nc.dram_tensor(nm, list(sh), dt, kind="ExternalInput")
    return dts



from concourse.bass_utils import run_bass_kernel_spmd

FUSED = True

_PASSTHRU = [
    "L0_attn_bq", "L0_attn_bk", "L0_attn_bv", "L0_attn_bo",
    "L0_attn_Wq", "L0_attn_Wk", "L0_attn_Wv",
    "L0_ffn_gate_b", "L0_ffn_b1", "L0_ffn_b3", "L0_ffn_b2",
    "L1_attn_bq", "L1_attn_bk1", "L1_attn_bk2", "L1_attn_bv1", "L1_attn_bv2",
    "L1_attn_bo", "L1_attn_g1", "L1_attn_g2", "L1_attn_g3", "L1_attn_g4",
    "L1_attn_ln_g", "L1_attn_ln_b",
    "L1_ffn_q_scale", "L1_ffn_k_scale", "L1_ffn_wo", "L1_ffn_ln_g", "L1_ffn_ln_b",
    "L1_ffn_w1", "L1_ffn_b1", "L1_ffn_w2", "L1_ffn_b2",
]

_BLOCKS = ("l0a", "l0f", "l1a", "l1f")

_DECL = {"l0a": decl_l0_attn_weights, "l0f": decl_l0_ffn_weights,
         "l1a": decl_l1_attn_weights, "l1f": decl_l1_ffn_weights}
_EMIT = {"l0a": emit_l0_attn, "l0f": emit_l0_ffn, "l1a": emit_l1_attn, "l1f": emit_l1_ffn}

_KEYS = {
    "l0a": ["L0_attn_Wq", "L0_attn_Wk", "L0_attn_Wv", "WoT", "L0_attn_bq", "L0_attn_bk",
            "L0_attn_bv", "L0_attn_bo", "A_q", "A_k", "A_v", "vb_ones", "cfl"],
    "l0f": ["gwT", "L0_ffn_gate_b", "w1T", "w3T", "w2T", "L0_ffn_b1", "L0_ffn_b3",
            "L0_ffn_b2", "ws1T", "ws3T", "ws2T", "bs1", "bs3", "bs2"],
    "l1a": ["WqT", "Wk1T", "Wk2T", "Wv1T", "Wv2T", "WoT_1", "L1_attn_bq", "L1_attn_bk1",
            "L1_attn_bk2", "L1_attn_bv1", "L1_attn_bv2", "L1_attn_bo", "L1_attn_g1",
            "L1_attn_g2", "L1_attn_g3", "L1_attn_g4", "L1_attn_ln_g", "L1_attn_ln_b"],
    "l1f": ["wqkv_aug", "vones_row", "csq2", "snq2", "csk2", "snk2", "c_rot2",
            "L1_ffn_q_scale", "L1_ffn_k_scale", "L1_ffn_wo", "L1_ffn_ln_g", "L1_ffn_ln_b",
            "L1_ffn_w1", "L1_ffn_b1", "L1_ffn_w2", "L1_ffn_b2"],
}


def _build_block(which):
    nc = bass.Bass("TRN2", target_bir_lowering=False)
    xin = nc.dram_tensor("xin", [1024, T], F32, kind="ExternalInput")
    xout = nc.dram_tensor("xout", [1024, T], F32, kind="ExternalOutput")
    dts = decl_consts(nc)
    W = _DECL[which](nc)
    with contextlib.ExitStack() as stack:
        S = Sync(nc, stack)
        C = Consts(nc, S, stack, dts)
        _EMIT[which](nc, S, C, stack, xin.ap(), xout.ap(), W)
        S.barrier()
    return nc


def _build_fused():
    nc = bass.Bass("TRN2", target_bir_lowering=False)
    xin = nc.dram_tensor("xin", [1024, T], F32, kind="ExternalInput")
    xout = nc.dram_tensor("xout", [1024, T], F32, kind="ExternalOutput")
    x1 = nc.dram_tensor("x_mid1", [1024, T], F32)
    x2 = nc.dram_tensor("x_mid2", [1024, T], F32)
    x3 = nc.dram_tensor("x_mid3", [1024, T], F32)
    dts = decl_consts(nc)
    W = {}
    for which in _BLOCKS:
        W.update(_DECL[which](nc))
    with contextlib.ExitStack() as stack:
        S = Sync(nc, stack)
        C = Consts(nc, S, stack, dts)
        emit_l0_attn(nc, S, C, stack, xin.ap(), x1.ap(), W)
        emit_l0_ffn(nc, S, C, stack, x1.ap(), x2.ap(), W)
        emit_l1_attn(nc, S, C, stack, x2.ap(), x3.ap(), W)
        emit_l1_ffn(nc, S, C, stack, x3.ap(), xout.ap(), W)
        S.barrier()
    return nc


def _host_inputs(inputs):
    host = {}
    host.update(host_constants())
    host.update(host_l0_attn_inputs(inputs))
    host.update(host_l0_ffn_inputs(inputs))
    host.update(host_l1_attn_inputs(inputs))
    host.update(host_l1_ffn_inputs(inputs))
    for k in _PASSTHRU:
        host[k] = np.ascontiguousarray(np.asarray(inputs[k], dtype=np.float32))
    return host


def kernel(**inputs):
    host = _host_inputs(inputs)
    x = np.asarray(inputs["L0_attn_hidden_states"], dtype=np.float32)
    xs = [np.ascontiguousarray(x[i].T) for i in range(8)]

    if FUSED:
        nc = _build_fused()
        base = dict(host)
        in_maps = [{**base, "xin": xs[i]} for i in range(8)]
        res = run_bass_kernel_spmd(nc, in_maps, core_ids=list(range(8)))
        out = np.stack([np.asarray(res.results[i]["xout"]).T for i in range(8)])
        return np.ascontiguousarray(out, dtype=np.float32)

    cur = xs
    for which in _BLOCKS:
        nc = _build_block(which)
        keys = _KEYS[which]
        base = {k: host[k] for k in keys}
        base.update({k: host[k] for k in host_constants()})
        in_maps = [{**base, "xin": cur[i]} for i in range(8)]
        res = run_bass_kernel_spmd(nc, in_maps, core_ids=list(range(8)))
        cur = [np.ascontiguousarray(np.asarray(res.results[i]["xout"])) for i in range(8)]
    out = np.stack([c.T for c in cur])
    return np.ascontiguousarray(out, dtype=np.float32)
